# version 01
# baseline (speedup 1.0000x reference)
import numpy as np
import ml_dtypes
from contextlib import ExitStack

import concourse.bass as bass
import concourse.mybir as mybir
from concourse.bass_utils import run_bass_kernel_spmd

F32 = mybir.dt.float32
BF16 = mybir.dt.bfloat16
U8 = mybir.dt.uint8
AF = mybir.ActivationFunctionType
ALU = mybir.AluOpType
AX = mybir.AxisListType
BF = ml_dtypes.bfloat16

B, S, D = 2, 4096, 4096
NH, HD, RDIM = 16, 256, 64
NE = 3277
TPC = 1024
NT = 8
NCORE = 8
MASKVAL = np.float32(-1.0e9 / 16.0)
INVD = 1.0 / 4096.0
EPS = 1e-5

_DEBUG = False
_STAGE = 5


def _host_tables():
    ident = np.eye(128, dtype=BF)
    prot = np.zeros((128, 128), dtype=np.float32)
    for j in range(32):
        prot[2 * j + 1, 2 * j] = -1.0
        prot[2 * j, 2 * j + 1] = 1.0
    prot = prot.astype(BF)
    cvec = np.empty((128, 32), dtype=np.float32)
    for ic in range(32):
        cvec[:, ic] = np.where(ic * 128 + np.arange(128) < NE, 2.0, -8.0)
    return ident, prot, cvec


def _host_rope(sub):
    pos = np.empty(TPC, dtype=np.float64)
    for k in range(NT):
        pos[k * 128:(k + 1) * 128] = k * 512 + sub * 128 + np.arange(128)
    j = np.arange(32, dtype=np.float64)
    inv_freq = 1.0 / (10000.0 ** (2.0 * j / RDIM))
    ang = inv_freq[:, None] * pos[None, :]
    sin = np.repeat(np.sin(ang), 2, axis=0)
    cos = np.repeat(np.cos(ang), 2, axis=0)
    return cos.astype(BF), sin.astype(BF)


def _host_mask(sub):
    m = np.zeros((128, 512), dtype=np.float32)
    qi = np.arange(128)[:, None]
    col = np.arange(512)[None, :]
    m[col > sub * 128 + qi] = MASKVAL
    return m


class Fl:

    def __init__(self, key, eng, sems):
        self.key = key
        self.eng = eng
        self.sems = sems
        self.c = {k: 0 for k in sems}

    def do(self, key, sem, amt, fn):
        self.c[sem] += amt
        if self.key == key:
            fn(self.eng).then_inc(self.sems[sem], amt)
        return self.c[sem]

    def nop(self, key, fn):
        if self.key == key:
            fn(self.eng)

    def wait(self, key, sem, val):
        if val is None or val <= 0:
            return
        if self.key == key:
            self.eng.wait_ge(self.sems[sem], val)

    def waitpair(self, key, pair):
        if pair is None:
            return
        (a, b) = pair
        self.wait(key, "st0", a)
        self.wait(key, "st1", b)

    def stpair(self):
        return (self.c["st0"], self.c["st1"])

    def cur(self, sem):
        return self.c[sem]

    def sop(self, key, fn):
        self.wait(key, key, self.c[key])
        return self.do(key, key, 1, fn)


def build_program():
    nc = bass.Bass("TRN2", target_bir_lowering=False, num_devices=8)

    x_d = nc.dram_tensor("x", [TPC, D], F32, kind="ExternalInput")
    wqT_d = nc.dram_tensor("wqT", [D, D], F32, kind="ExternalInput")
    wkT_d = nc.dram_tensor("wkT", [D, D], F32, kind="ExternalInput")
    wvT_d = nc.dram_tensor("wvT", [D, D], F32, kind="ExternalInput")
    woT_d = nc.dram_tensor("woT", [D, D], F32, kind="ExternalInput")
    k0_d = nc.dram_tensor("k0", [D, D], U8, kind="ExternalInput")
    k1_d = nc.dram_tensor("k1", [D, D], U8, kind="ExternalInput")
    k2_d = nc.dram_tensor("k2", [D, D], U8, kind="ExternalInput")
    sc_d = nc.dram_tensor("sc", [128, 3], F32, kind="ExternalInput")
    ropec_d = nc.dram_tensor("ropec", [64, TPC], BF16, kind="ExternalInput")
    ropes_d = nc.dram_tensor("ropes", [64, TPC], BF16, kind="ExternalInput")
    mask_d = nc.dram_tensor("maskt", [128, 512], F32, kind="ExternalInput")
    out_d = nc.dram_tensor("out", [TPC, D], F32, kind="ExternalOutput")

    ident_np, prot_np, cvec_np = _host_tables()
    ident_d = nc.inline_tensor(ident_np, "ident")
    prot_d = nc.inline_tensor(prot_np, "prot")
    cvec_d = nc.inline_tensor(cvec_np, "cvec")

    qsp_d = nc.dram_tensor("qsp", [D, TPC], BF16, kind="Internal")
    kc_d = nc.dram_tensor("kcon", [D, TPC], BF16, kind="Internal")
    vc_d = nc.dram_tensor("vcon", [TPC, D], BF16, kind="Internal")
    kg_d = nc.dram_tensor("kgat", [4 * D, TPC], BF16, kind="Internal")
    vg_d = nc.dram_tensor("vgat", [4 * TPC, D], BF16, kind="Internal")
    r1_d = nc.dram_tensor("r1d", [TPC, D], F32, kind="Internal")

    dbg = {}
    if _DEBUG:
        for nm, shp, dt in [
            ("dbg_arena1", [128, 32768], BF16), ("dbg_qsp", [D, TPC], BF16),
            ("dbg_kg", [4 * D, TPC], BF16), ("dbg_vg", [4 * TPC, D], BF16),
            ("dbg_r1", [TPC, D], F32),
        ]:
            dbg[nm] = nc.dram_tensor(nm, shp, dt, kind="ExternalOutput")

    groups = [[0, 1, 2, 3], [4, 5, 6, 7]]

    with ExitStack() as ctx:
        E = ctx.enter_context
        A1 = E(nc.sbuf_tensor("arena1", [128, 65536], BF16))
        A2 = E(nc.sbuf_tensor("arena2", [128, 32768], BF16))
        cols = E(nc.sbuf_tensor("cols", [128, 16], F32))
        scb = E(nc.sbuf_tensor("scb", [128, 3], F32))
        ds0 = E(nc.sbuf_tensor("ds0", [128, 2], F32))
        cvec = E(nc.sbuf_tensor("cvec_sb", [128, 32], F32))
        cs1a = E(nc.sbuf_tensor("cs1a", [128, 32], F32))
        cs1b = E(nc.sbuf_tensor("cs1b", [128, 32], F32))
        cs2a = E(nc.sbuf_tensor("cs2a", [128, 32], F32))
        cs2b = E(nc.sbuf_tensor("cs2b", [128, 32], F32))
        ident = E(nc.sbuf_tensor("ident_sb", [128, 128], BF16))
        prot = E(nc.sbuf_tensor("prot_sb", [128, 128], BF16))
        maskt = E(nc.sbuf_tensor("mask_sb", [128, 512], F32))
        ropec = E(nc.sbuf_tensor("ropec_sb", [64, TPC], BF16))
        ropes = E(nc.sbuf_tensor("ropes_sb", [64, TPC], BF16))
        xsl = [E(nc.sbuf_tensor("xslA", [128, 512], F32)),
               E(nc.sbuf_tensor("xslB", [128, 512], F32))]
        fst = [E(nc.sbuf_tensor("fstA", [128, 512], F32)),
               E(nc.sbuf_tensor("fstB", [128, 512], F32))]
        kconst = E(nc.sbuf_tensor("kconst", [128, 2], F32))

        XF = A1[:, 0:32768]
        ATTN = A1[:, 0:32768]
        XF2 = A1[:, 0:32768]
        H2 = A1[:, 0:32768]
        WSs = [A1[:, 32768:36864], A1[:, 36864:40960]]
        WSb = [A1[:, 32768:49152], A1[:, 49152:65536]]
        KH = [A1[:, 32768:40960], A1[:, 40960:49152]]
        VH = [A1[:, 49152:57344], A1[:, 57344:65536]]

        a2f32 = A2.bitcast(F32)
        xt = [a2f32[:, 0:4096], a2f32[:, 4096:8192]]
        SQ = a2f32[:, 8192:12288]
        Sbuf = a2f32[:, 0:4096]
        Abuf = A2[:, 16384:20480]
        QT = [A2[:, 20480:20736], A2[:, 20736:20992]]
        ATsb = [A2[:, 20992:21504], A2[:, 21504:22016]]
        attn_st = A2[:, 22016:22272]
        xlnbf = A2[:, 24576:28672]
        t64a = a2f32[:, 14336:14848]
        t64b = a2f32[:, 14848:15360]
        rotbf = A2[:, 30720:31232]
        stbf = [A2[:, 31232:31744], A2[:, 31744:32256]]
        H1 = A2[:, 0:32768]

        pbA = E(nc.psum_tensor("pbA", [128, 512], F32))
        pbB = E(nc.psum_tensor("pbB", [128, 512], F32))
        pbC = E(nc.psum_tensor("pbC", [128, 512], F32))
        pbE = E(nc.psum_tensor("pbE", [128, 1024], BF16))
        pbF = E(nc.psum_tensor("pbF", [128, 1024], BF16))
        pmm = [pbA, pbB]
        ptp = [pbE, pbF]

        semnames = ["sdc", "xl0", "xl1", "wl0", "wl1", "kv0", "kv1",
                    "qt0", "qt1", "sl0", "sl1", "st0", "st1",
                    "te", "ve", "sc", "co"]
        sems = {n: E(nc.semaphore(n)) for n in semnames}

        blk = E(nc.Block())

        def flow(f: Fl):
            for dst, src in [
                (ident[:, :], ident_d[:, :]), (prot[:, :], prot_d[:, :]),
                (cvec[:, :], cvec_d[:, :]), (maskt[:, :], mask_d[:, :]),
                (ropec[:, :], ropec_d[:, :]), (ropes[:, :], ropes_d[:, :]),
                (scb[:, :], sc_d[:, :]),
            ]:
                f.do("sy", "sdc", 16, lambda e, d=dst, s=src: e.dma_start(out=d, in_=s))
            sd_const = f.cur("sdc")
            for k_ in ("te", "sc", "gp", "ve"):
                f.wait(k_, "sdc", sd_const)
            f.sop("ve", lambda e: e.memset(kconst[:, 0:1], EPS))
            f.sop("ve", lambda e: e.memset(kconst[:, 1:2], 0.0625))
            for (o, i, s1) in [(ds0[:, 0:1], scb[:, 0:1], 2.0),
                               (ds0[:, 1:2], scb[:, 0:1], -1.0)]:
                f.sop("ve", lambda e, o=o, i=i, s1=s1: e.tensor_scalar(
                    out=o, in0=i, scalar1=s1, scalar2=None, op0=ALU.mult))
            for (o, sidx, s2) in [(cs1a, 1, 2.0), (cs1b, 1, -1.0),
                                  (cs2a, 2, 2.0), (cs2b, 2, -1.0)]:
                f.sop("ve", lambda e, o=o, sidx=sidx, s2=s2: e.tensor_scalar(
                    out=o[:, :], in0=cvec[:, :], scalar1=scb[:, sidx:sidx + 1],
                    scalar2=s2, op0=ALU.mult, op1=ALU.mult))
            v_const = f.sop("ve",
                           lambda e: e.tensor_copy(out=cols[:, 15:16], in_=scb[:, 0:1]))

            def ln_pass(src_dram, dstFM, stgate):
                loads = []
                lastev_tp = [None, None]
                prev_xln = None
                for t in range(2):
                    f.waitpair("sy", stgate)
                    if t == 0:
                        f.wait("sy", "ve", v_const)
                    loads.append(f.do("sy", "xl%d" % (t % 2), 16,
                                      lambda e, t=t: e.dma_start(
                                          out=xt[t % 2][:, :],
                                          in_=src_dram[t * 128:(t + 1) * 128, :])))
                for t in range(NT):
                    xtile = xt[t % 2]
                    xls = "xl%d" % (t % 2)
                    f.wait("ve", xls, loads[t])
                    f.sop("ve", lambda e, xtile=xtile: e.reduce_sum(
                        out=cols[:, 1:2], in_=xtile[:, :], axis=AX.X))
                    f.sop("ve", lambda e, xtile=xtile: e.tensor_tensor(
                        out=SQ[:, :], in0=xtile[:, :], in1=xtile[:, :], op=ALU.mult))
                    f.sop("ve", lambda e: e.reduce_sum(
                        out=cols[:, 0:1], in_=SQ[:, :], axis=AX.X))
                    f.sop("ve", lambda e: e.tensor_scalar(
                        out=cols[:, 2:3], in0=cols[:, 1:2], scalar1=INVD,
                        scalar2=None, op0=ALU.mult))
                    f.sop("ve", lambda e: e.tensor_tensor(
                        out=cols[:, 3:4], in0=cols[:, 2:3], in1=cols[:, 2:3],
                        op=ALU.mult))
                    f.sop("ve", lambda e: e.scalar_tensor_tensor(
                        out=cols[:, 4:5], in0=cols[:, 0:1], scalar=INVD,
                        in1=cols[:, 3:4], op0=ALU.mult, op1=ALU.subtract))
                    v_var = f.sop("ve",
                                 lambda e: e.tensor_copy(out=cols[:, 11:12],
                                                         in_=cols[:, 4:5]))
                    f.wait("sc", "ve", v_var)
                    s_std = f.sop("sc",
                                 lambda e: e.activation(out=cols[:, 5:6], in_=cols[:, 4:5],
                                                        func=AF.Sqrt, bias=kconst[:, 0:1]))
                    f.wait("ve", "sc", s_std)
                    f.sop("ve", lambda e: e.reciprocal(out=cols[:, 6:7], in_=cols[:, 5:6]))
                    f.wait("ve", "te", prev_xln)
                    v_xln = f.sop("ve",
                                 lambda e, xtile=xtile: e.tensor_scalar(
                                     out=xlnbf[:, :], in0=xtile[:, :],
                                     scalar1=cols[:, 2:3], scalar2=cols[:, 6:7],
                                     op0=ALU.subtract, op1=ALU.mult))
                    if t + 2 < NT:
                        f.wait("sy", "ve", v_xln)
                        loads.append(f.do("sy", xls, 16,
                                          lambda e, t=t: e.dma_start(
                                              out=xt[t % 2][:, :],
                                              in_=src_dram[(t + 2) * 128:(t + 3) * 128, :])))
                    f.wait("te", "ve", v_xln)
                    t_last = None
                    for g in range(8):
                        bank = ptp[g % 2]
                        f.wait("te", "ve", lastev_tp[g % 2])
                        def tgrp(e, g=g, bank=bank):
                            for j in range(4):
                                dc = g * 4 + j
                                i = e.transpose(bank[:, j * 128:(j + 1) * 128],
                                                xlnbf[:, dc * 128:(dc + 1) * 128],
                                                ident[:, :])
                            return i
                        t_grp = f.do("te", "te", 1, tgrp)
                        t_last = t_grp
                        f.wait("ve", "te", t_grp)
                        def evac(e, g=g, bank=bank, t=t):
                            dst = dstFM.rearrange("p (dc tt) -> p dc tt", dc=32)[
                                :, g * 4:(g + 1) * 4, t * 128:(t + 1) * 128]
                            return e.tensor_copy(
                                out=dst, in_=bank[:, 0:512].rearrange(
                                    "p (j c) -> p j c", j=4))
                        lastev_tp[g % 2] = f.sop("ve", evac)
                    prev_xln = t_last
                return f.cur("ve")

            v_xf_done = ln_pass(x_d, XF, None)

            if _STAGE <= 1:
                if _DEBUG:
                    f.wait("gp", "ve", v_xf_done)
                    f.do("gp", "st0", 16, lambda e: e.dma_start(
                        out=dbg["dbg_arena1"][:, :], in_=A1[:, 0:32768]))
                f.waitpair("gp", f.stpair())
                f.waitpair("sy", f.stpair())
                return

            def fm_proj(wT_d, dst_dram, do_rope, do_scale, v_in_ready):
                te0 = f.cur("te")
                last_mm_slab = [None, None]
                last_ev_bank = [None, None]
                last_st_stage = [f.cur("st0"), f.cur("st1")]
                last_v_rot = None
                for ms in range(32):
                    buf = WSs[ms % 2]
                    wls = "wl%d" % (ms % 2)
                    f.wait("gp", "te", last_mm_slab[ms % 2] or te0)
                    src = wT_d[:, ms * 128:(ms + 1) * 128].rearrange(
                        "(dc p) m -> p dc m", p=128)
                    dst3 = buf.rearrange("p (dc m) -> p dc m", m=128)
                    gl = f.do("gp", wls, 16,
                              lambda e, dst3=dst3, src=src: e.dma_start(out=dst3, in_=src))
                    for ts2 in range(2):
                        bank = pmm[ts2]
                        sts = "st%d" % ts2
                        f.wait("te", wls, gl)
                        if ms == 0 and ts2 == 0:
                            f.wait("te", "ve", v_in_ready)
                        f.wait("te", "sc", last_ev_bank[ts2])
                        def mm(e, buf=buf, bank=bank, ts2=ts2):
                            for dc in range(32):
                                i = e.matmul(
                                    bank[:, :], buf[:, dc * 128:(dc + 1) * 128],
                                    XF[:, dc * 1024 + ts2 * 512: dc * 1024 + ts2 * 512 + 512],
                                    start=(dc == 0), stop=(dc == 31))
                            return i
                        t_mm = f.do("te", "te", 1, mm)
                        last_mm_slab[ms % 2] = t_mm
                        st = stbf[ts2]
                        f.wait("sc", sts, last_st_stage[ts2])
                        f.wait("sc", "te", t_mm)
                        s_ev = f.sop("sc",
                                    lambda e, st=st, bank=bank, do_scale=do_scale: e.activation(
                                        out=st[:, :], in_=bank[:, :], func=AF.Copy,
                                        scale=(kconst[:, 1:2] if do_scale else 1.0)))
                        last_ev_bank[ts2] = s_ev
                        prod_sem, prod_val = "sc", s_ev
                        if do_rope and (ms % 2 == 0):
                            f.wait("te", "sc", s_ev)
                            f.wait("te", "ve", last_v_rot)
                            t_rot = f.do("te", "te", 1,
                                         lambda e, st=st: e.matmul(
                                             pbC[:, :], prot[:, :], st[:, :],
                                             start=True, stop=True))
                            f.wait("sc", "te", t_rot)
                            s_rb = f.sop("sc",
                                        lambda e: e.activation(out=rotbf[0:64, :],
                                                               in_=pbC[0:64, :],
                                                               func=AF.Copy))
                            last_ev_bank[ts2] = s_rb
                            f.wait("ve", "sc", s_ev)
                            f.sop("ve", lambda e, st=st, ts2=ts2: e.tensor_tensor(
                                out=t64a[0:64, :], in0=st[0:64, :],
                                in1=ropec[:, ts2 * 512:(ts2 + 1) * 512], op=ALU.mult))
                            f.wait("ve", "sc", s_rb)
                            f.sop("ve", lambda e, ts2=ts2: e.tensor_tensor(
                                out=t64b[0:64, :], in0=rotbf[0:64, :],
                                in1=ropes[:, ts2 * 512:(ts2 + 1) * 512], op=ALU.mult))
                            v_rot = f.sop("ve",
                                         lambda e, st=st: e.tensor_tensor(
                                             out=st[0:64, :], in0=t64a[0:64, :],
                                             in1=t64b[0:64, :], op=ALU.add))
                            last_v_rot = v_rot
                            prod_sem, prod_val = "ve", v_rot
                        f.wait("gp", prod_sem, prod_val)
                        g_st = f.do("gp", sts, 16,
                                    lambda e, st=st, ms=ms, ts2=ts2, dd=dst_dram: e.dma_start(
                                        out=dd[ms * 128:(ms + 1) * 128,
                                               ts2 * 512:(ts2 + 1) * 512],
                                        in_=st[:, :]))
                        last_st_stage[ts2] = g_st
                return f.stpair()

            kp = fm_proj(wkT_d, kc_d, do_rope=True, do_scale=False,
                         v_in_ready=v_xf_done)
            f.waitpair("gp", kp)
            co_k = f.do("gp", "co", 1,
                        lambda e: e.collective_compute(
                            "AllGather", ALU.bypass, replica_groups=groups,
                            ins=[kc_d.ap().opt()], outs=[kg_d.ap().opt()]))

            def tm_proj(wT_d, dst_dram, src_act, r1_add, te_ready, ve_ready):
                te0 = f.cur("te")
                last_mm_slab = [None, None]
                last_ev_bank = [None, None]
                last_st_stage = [f.cur("st0"), f.cur("st1")]
                last_sl_ev = [None, None]
                evsem = "ve" if r1_add is not None else "sc"
                for ms in range(8):
                    buf = WSb[ms % 2]
                    wls = "wl%d" % (ms % 2)
                    f.wait("gp", "te", last_mm_slab[ms % 2] or te0)
                    src = wT_d[:, ms * 512:(ms + 1) * 512].rearrange(
                        "(dc p) m -> p dc m", p=128)
                    dst3 = buf.rearrange("p (dc m) -> p dc m", m=512)
                    gl = f.do("gp", wls, 16,
                              lambda e, dst3=dst3, src=src: e.dma_start(out=dst3, in_=src))
                    for tt in range(NT):
                        bank = pmm[tt % 2]
                        sts = "st%d" % (tt % 2)
                        sls = "sl%d" % (tt % 2)
                        sl = None
                        if r1_add is not None:
                            xs = xsl[tt % 2]
                            f.wait("sy", "ve", last_sl_ev[tt % 2])
                            sl = f.do("sy", sls, 16,
                                      lambda e, xs=xs, ms=ms, tt=tt: e.dma_start(
                                          out=xs[:, :],
                                          in_=r1_add[tt * 128:(tt + 1) * 128,
                                                     ms * 512:(ms + 1) * 512]))
                        f.wait("te", wls, gl)
                        if ms == 0 and tt == 0:
                            f.wait("te", "ve", ve_ready)
                            f.wait("te", "te", te_ready)
                        f.wait("te", evsem, last_ev_bank[tt % 2])
                        def mmv(e, buf=buf, bank=bank, tt=tt):
                            for dc in range(32):
                                i = e.matmul(
                                    bank[:, :],
                                    src_act[:, dc * 1024 + tt * 128: dc * 1024 + tt * 128 + 128],
                                    buf[:, dc * 512:(dc + 1) * 512],
                                    start=(dc == 0), stop=(dc == 31))
                            return i
                        t_mm = f.do("te", "te", 1, mmv)
                        last_mm_slab[ms % 2] = t_mm
                        if r1_add is None:
                            st = stbf[tt % 2]
                            f.wait("sc", sts, last_st_stage[tt % 2])
                            f.wait("sc", "te", t_mm)
                            s_ev = f.sop("sc",
                                        lambda e, st=st, bank=bank: e.activation(
                                            out=st[:, :], in_=bank[:, :], func=AF.Copy))
                            last_ev_bank[tt % 2] = s_ev
                            f.wait("gp", "sc", s_ev)
                            g_st = f.do("gp", sts, 16,
                                        lambda e, st=st, ms=ms, tt=tt: e.dma_start(
                                            out=dst_dram[tt * 128:(tt + 1) * 128,
                                                         ms * 512:(ms + 1) * 512],
                                            in_=st[:, :]))
                        else:
                            fs = fst[tt % 2]
                            f.wait("ve", sts, last_st_stage[tt % 2])
                            f.wait("ve", "te", t_mm)
                            f.wait("ve", sls, sl)
                            v_ev = f.sop("ve",
                                        lambda e, bank=bank, xs=xs, fs=fs: e.scalar_tensor_tensor(
                                            out=fs[:, :], in0=bank[:, :], scalar=1.0,
                                            in1=xs[:, :], op0=ALU.mult, op1=ALU.add))
                            last_ev_bank[tt % 2] = v_ev
                            last_sl_ev[tt % 2] = v_ev
                            f.wait("gp", "ve", v_ev)
                            g_st = f.do("gp", sts, 16,
                                        lambda e, fs=fs, ms=ms, tt=tt: e.dma_start(
                                            out=dst_dram[tt * 128:(tt + 1) * 128,
                                                         ms * 512:(ms + 1) * 512],
                                            in_=fs[:, :]))
                        last_st_stage[tt % 2] = g_st
                return f.stpair()

            vp = tm_proj(wvT_d, vc_d, XF, None, 0, v_xf_done)
            f.waitpair("gp", vp)
            co_v = f.do("gp", "co", 1,
                        lambda e: e.collective_compute(
                            "AllGather", ALU.bypass, replica_groups=groups,
                            ins=[vc_d.ap().opt()], outs=[vg_d.ap().opt()]))

            qp = fm_proj(wqT_d, qsp_d, do_rope=True, do_scale=True,
                         v_in_ready=v_xf_done)
            te_q_done = f.cur("te")

            if _STAGE <= 2:
                if _DEBUG:
                    f.waitpair("gp", f.stpair())
                    f.wait("gp", "co", f.cur("co"))
                    f.wait("gp", "te", te_q_done)
                    for nm, src in [("dbg_arena1", A1[:, 0:32768]),
                                    ("dbg_qsp", qsp_d[:, :]),
                                    ("dbg_kg", kg_d[:, :]), ("dbg_vg", vg_d[:, :])]:
                        f.do("gp", "st0", 16,
                             lambda e, nm=nm, src=src: e.dma_start(out=dbg[nm][:, :], in_=src))
                f.waitpair("gp", f.stpair())
                f.waitpair("sy", f.stpair())
                return

            kv_loaded = {}
            kv_users = {}

            def load_head(h):
                par = h % 2
                kvs = "kv%d" % par
                f.wait("sy", "te", kv_users.get(h - 2))
                if h <= 1:
                    f.wait("sy", "co", co_v)
                    f.wait("sy", "te", te_q_done)
                    f.waitpair("sy", qp)
                last = None
                for dc in range(2):
                    for sb in range(4):
                        srcrow = sb * 4096 + h * 256 + dc * 128
                        src = kg_d[srcrow:srcrow + 128, :].rearrange(
                            "p (kc j) -> p kc j", kc=8)
                        dst = KH[par][:, dc * 4096:(dc + 1) * 4096].rearrange(
                            "p (kc four j) -> p kc four j", four=4, j=128)[:, :, sb, :]
                        last = f.do("sy", kvs, 16,
                                    lambda e, d=dst, s=src: e.dma_start(out=d, in_=s))
                for sb in range(4):
                    for kcc in range(8):
                        srcrow = sb * 1024 + kcc * 128
                        piece = kcc * 4 + sb
                        last = f.do("sy", kvs, 16,
                                    lambda e, srcrow=srcrow, piece=piece, par=par, h=h:
                                    e.dma_start(
                                        out=VH[par][:, piece * 256:(piece + 1) * 256],
                                        in_=vg_d[srcrow:srcrow + 128,
                                                 h * 256:(h + 1) * 256]))
                kv_loaded[h] = last

            load_head(0)
            load_head(1)
            last_qt_user = [None, None]
            last_sc_ev = [None, None]
            last_at_ev = [None, None]
            prev_s_exp = None
            prev_t_at_last = None
            prev_v_attn = None
            for h in range(NH):
                par = h % 2
                kvs = "kv%d" % par
                t_av = None
                for k in range(NT):
                    qpar = k % 2
                    qts = "qt%d" % qpar
                    f.wait("sy", "te", last_qt_user[qpar])
                    ql = None
                    for dc in range(2):
                        ql = f.do("sy", qts, 16,
                                  lambda e, h=h, k=k, qpar=qpar, dc=dc: e.dma_start(
                                      out=QT[qpar][:, dc * 128:(dc + 1) * 128],
                                      in_=qsp_d[h * 256 + dc * 128: h * 256 + dc * 128 + 128,
                                                k * 128:(k + 1) * 128]))
                    for kc in range(k + 1):
                        bank = pmm[kc % 2]
                        f.wait("te", "ve", last_sc_ev[kc % 2])
                        f.wait("te", qts, ql)
                        if kc == 0:
                            f.wait("te", kvs, kv_loaded[h])
                        def smm(e, bank=bank, kc=kc, qpar=qpar, par=par):
                            for dc in range(2):
                                i = e.matmul(
                                    bank[:, :], QT[qpar][:, dc * 128:(dc + 1) * 128],
                                    KH[par][:, dc * 4096 + kc * 512: dc * 4096 + kc * 512 + 512],
                                    start=(dc == 0), stop=(dc == 1))
                            return i
                        t_s = f.do("te", "te", 1, smm)
                        f.wait("ve", "te", t_s)
                        if kc == 0:
                            f.wait("ve", "sc", prev_s_exp)
                        if kc == k:
                            v_ev = f.sop("ve",
                                        lambda e, bank=bank, kc=kc: e.tensor_tensor(
                                            out=Sbuf[:, kc * 512:(kc + 1) * 512],
                                            in0=bank[:, :], in1=maskt[:, :], op=ALU.add))
                        else:
                            v_ev = f.sop("ve",
                                        lambda e, bank=bank, kc=kc: e.tensor_copy(
                                            out=Sbuf[:, kc * 512:(kc + 1) * 512],
                                            in_=bank[:, :]))
                        last_sc_ev[kc % 2] = v_ev
                    L = (k + 1) * 512
                    f.sop("ve", lambda e, L=L: e.reduce_max(
                        out=cols[:, 8:9], in_=Sbuf[:, 0:L],
                        axis=AX.X, negate=True))
                    v_m = f.sop("ve",
                               lambda e: e.tensor_copy(out=cols[:, 11:12],
                                                       in_=cols[:, 8:9]))
                    f.wait("sc", "ve", v_m)
                    f.wait("sc", "te", prev_t_at_last)
                    s_exp = f.sop("sc",
                                 lambda e, L=L: e.activation(
                                     out=Abuf[:, 0:L], in_=Sbuf[:, 0:L], func=AF.Exp,
                                     bias=cols[:, 8:9]))
                    prev_s_exp = s_exp
                    f.wait("ve", "sc", s_exp)
                    f.sop("ve", lambda e, L=L: e.reduce_sum(
                        out=cols[:, 9:10], in_=Abuf[:, 0:L], axis=AX.X))
                    v_rinv = f.sop("ve",
                                  lambda e: e.reciprocal(out=cols[:, 10:11],
                                                         in_=cols[:, 9:10]))
                    t_at = None
                    for kc in range(k + 1):
                        tb = ptp[kc % 2]
                        f.wait("te", "ve", last_at_ev[kc % 2])
                        if kc == 0:
                            f.wait("te", "sc", s_exp)
                            f.wait("te", "ve", prev_v_attn)
                        def tat(e, tb=tb, kc=kc):
                            for j in range(4):
                                c0 = kc * 512 + j * 128
                                i = e.transpose(tb[:, j * 128:(j + 1) * 128],
                                                Abuf[:, c0:c0 + 128], ident[:, :])
                            return i
                        t_at = f.do("te", "te", 1, tat)
                        f.wait("ve", "te", t_at)
                        v_atev = f.sop("ve",
                                      lambda e, tb=tb, kc=kc: e.tensor_copy(
                                          out=ATsb[kc % 2][:, :], in_=tb[:, 0:512]))
                        last_at_ev[kc % 2] = v_atev
                        f.wait("te", "ve", v_atev)
                        def avm(e, kc=kc, par=par, k=k):
                            for j in range(4):
                                piece = kc * 4 + j
                                i = e.matmul(
                                    pbC[:, 0:256], ATsb[kc % 2][:, j * 128:(j + 1) * 128],
                                    VH[par][:, piece * 256:(piece + 1) * 256],
                                    start=(kc == 0 and j == 0),
                                    stop=(kc == k and j == 3))
                            return i
                        t_av = f.do("te", "te", 1, avm)
                    prev_t_at_last = t_at
                    last_qt_user[qpar] = t_av
                    f.wait("ve", "te", t_av)
                    v_at = f.sop("ve",
                                lambda e: e.tensor_scalar(
                                    out=attn_st[:, :], in0=pbC[:, 0:256],
                                    scalar1=cols[:, 10:11], scalar2=None, op0=ALU.mult))
                    f.wait("te", "ve", v_at)
                    def tattn(e):
                        for dh in range(2):
                            i = e.transpose(pbE[:, 512 + dh * 128: 512 + (dh + 1) * 128],
                                            attn_st[:, dh * 128:(dh + 1) * 128],
                                            ident[:, :])
                        return i
                    t_ta = f.do("te", "te", 1, tattn)
                    f.wait("ve", "te", t_ta)
                    def evat(e, h=h, k=k):
                        dst = ATTN.rearrange("p (hc t) -> p hc t", hc=32)[
                            :, 2 * h:2 * h + 2, k * 128:(k + 1) * 128]
                        return e.tensor_copy(out=dst, in_=pbE[:, 512:768].rearrange(
                            "p (dh c) -> p dh c", dh=2))
                    prev_v_attn = f.sop("ve", evat)
                kv_users[h] = t_av
                if h + 2 < NH:
                    load_head(h + 2)
            v_attn_done = f.cur("ve")
            te_attn_done = f.cur("te")

            r1p = tm_proj(woT_d, r1_d, ATTN, x_d, te_attn_done, v_attn_done)

            v_xf2_done = ln_pass(r1_d, XF2, r1p)

            if _STAGE <= 4:
                if _DEBUG:
                    f.waitpair("gp", f.stpair())
                    f.wait("gp", "ve", v_xf2_done)
                    for nm, src in [("dbg_qsp", qsp_d[:, :]), ("dbg_kg", kg_d[:, :]),
                                    ("dbg_vg", vg_d[:, :]), ("dbg_r1", r1_d[:, :]),
                                    ("dbg_arena1", A1[:, 0:32768])]:
                        f.do("gp", "st0", 16,
                             lambda e, nm=nm, src=src: e.dma_start(out=dbg[nm][:, :], in_=src))
                f.waitpair("gp", f.stpair())
                f.waitpair("sy", f.stpair())
                return

            def ffn_fm(kx_d, sa, sb_, src_act, dstH, sc_in_ready, ve_in_ready):
                te0 = f.cur("te")
                last_mm_slab = [None, None]
                last_ev_bank = [None, None]
                for jsl in range(8):
                    buf = WSb[jsl % 2]
                    wls = "wl%d" % (jsl % 2)
                    f.wait("gp", "te", last_mm_slab[jsl % 2] or te0)
                    src = kx_d[:, jsl * 512:(jsl + 1) * 512].rearrange(
                        "(dc p) m -> p dc m", p=128)
                    dst3 = buf.rearrange("p (dc m) -> p dc m", m=512)
                    gl = f.do("gp", wls, 16,
                              lambda e, dst3=dst3, src=src: e.dma_start(out=dst3, in_=src))
                    f.wait("ve", wls, gl)
                    f.wait("ve", "te", last_mm_slab[jsl % 2] or te0)
                    def conv(e, buf=buf):
                        for ic in range(32):
                            s1 = sa[:, ic:ic + 1] if sa.shape[1] == 32 else sa
                            s2 = sb_[:, ic:ic + 1] if sb_.shape[1] == 32 else sb_
                            i = e.tensor_scalar(out=buf[:, ic * 512:(ic + 1) * 512],
                                                in0=buf[:, ic * 512:(ic + 1) * 512],
                                                scalar1=s1, scalar2=s2,
                                                op0=ALU.mult, op1=ALU.add)
                        return i
                    v_cv = f.sop("ve", conv)
                    for j1 in range(4):
                        for ts2 in range(2):
                            bank = pmm[ts2]
                            f.wait("te", "ve", v_cv)
                            if jsl == 0 and j1 == 0 and ts2 == 0:
                                f.wait("te", "sc", sc_in_ready)
                                f.wait("te", "ve", ve_in_ready)
                            f.wait("te", "sc", last_ev_bank[ts2])
                            def mmf(e, buf=buf, bank=bank, j1=j1, ts2=ts2):
                                for ic in range(32):
                                    i = e.matmul(
                                        bank[:, :],
                                        buf[:, ic * 512 + j1 * 128: ic * 512 + j1 * 128 + 128],
                                        src_act[:, ic * 1024 + ts2 * 512: ic * 1024 + ts2 * 512 + 512],
                                        start=(ic == 0), stop=(ic == 31))
                                return i
                            t_mm = f.do("te", "te", 1, mmf)
                            last_mm_slab[jsl % 2] = t_mm
                            f.wait("sc", "te", t_mm)
                            jc = jsl * 4 + j1
                            s_ev = f.sop("sc",
                                        lambda e, bank=bank, jc=jc, ts2=ts2: e.activation(
                                            out=dstH[:, jc * 1024 + ts2 * 512:
                                                     jc * 1024 + ts2 * 512 + 512],
                                            in_=bank[:, :], func=AF.Relu))
                            last_ev_bank[ts2] = s_ev
                return f.cur("sc")

            sc_w0_done = ffn_fm(k0_d, ds0[:, 0:1], ds0[:, 1:2], XF2, H1,
                                f.cur("sc"), v_xf2_done)
            sc_w1_done = ffn_fm(k1_d, cs1a, cs1b, H1, H2, sc_w0_done, 0)

            def ffn_out():
                te0 = f.cur("te")
                last_mm_slab = [None, None]
                last_ev_bank = [None, None]
                last_st_stage = [f.cur("st0"), f.cur("st1")]
                last_sl_ev = [None, None]
                for jsl in range(8):
                    buf = WSb[jsl % 2]
                    wls = "wl%d" % (jsl % 2)
                    f.wait("gp", "te", last_mm_slab[jsl % 2] or te0)
                    src = k2_d[:, jsl * 512:(jsl + 1) * 512].rearrange(
                        "(dc p) m -> p dc m", p=128)
                    dst3 = buf.rearrange("p (dc m) -> p dc m", m=512)
                    gl = f.do("gp", wls, 16,
                              lambda e, dst3=dst3, src=src: e.dma_start(out=dst3, in_=src))
                    f.wait("ve", wls, gl)
                    f.wait("ve", "te", last_mm_slab[jsl % 2] or te0)
                    def conv(e, buf=buf):
                        for ic in range(32):
                            i = e.tensor_scalar(out=buf[:, ic * 512:(ic + 1) * 512],
                                                in0=buf[:, ic * 512:(ic + 1) * 512],
                                                scalar1=cs2a[:, ic:ic + 1],
                                                scalar2=cs2b[:, ic:ic + 1],
                                                op0=ALU.mult, op1=ALU.add)
                        return i
                    v_cv = f.sop("ve", conv)
                    for tt in range(NT):
                        bank = pmm[tt % 2]
                        sts = "st%d" % (tt % 2)
                        sls = "sl%d" % (tt % 2)
                        rs = xsl[tt % 2]
                        f.wait("sy", "ve", last_sl_ev[tt % 2])
                        if jsl == 0 and tt <= 1:
                            f.waitpair("sy", r1p)
                        sl = f.do("sy", sls, 16,
                                  lambda e, rs=rs, jsl=jsl, tt=tt: e.dma_start(
                                      out=rs[:, :],
                                      in_=r1_d[tt * 128:(tt + 1) * 128,
                                               jsl * 512:(jsl + 1) * 512]))
                        f.wait("te", "ve", v_cv)
                        if jsl == 0 and tt == 0:
                            f.wait("te", "sc", sc_w1_done)
                        f.wait("te", "ve", last_ev_bank[tt % 2])
                        def mmf2(e, buf=buf, bank=bank, tt=tt):
                            for ic in range(32):
                                i = e.matmul(
                                    bank[:, :],
                                    H2[:, ic * 1024 + tt * 128: ic * 1024 + tt * 128 + 128],
                                    buf[:, ic * 512:(ic + 1) * 512],
                                    start=(ic == 0), stop=(ic == 31))
                            return i
                        t_mm = f.do("te", "te", 1, mmf2)
                        last_mm_slab[jsl % 2] = t_mm
                        fs = fst[tt % 2]
                        f.wait("ve", sts, last_st_stage[tt % 2])
                        f.wait("ve", "te", t_mm)
                        f.wait("ve", sls, sl)
                        v_ev = f.sop("ve",
                                    lambda e, bank=bank, rs=rs, fs=fs: e.scalar_tensor_tensor(
                                        out=fs[:, :], in0=bank[:, :], scalar=1.0,
                                        in1=rs[:, :], op0=ALU.mult, op1=ALU.add))
                        last_ev_bank[tt % 2] = v_ev
                        last_sl_ev[tt % 2] = v_ev
                        f.wait("gp", "ve", v_ev)
                        g_st = f.do("gp", sts, 16,
                                    lambda e, fs=fs, jsl=jsl, tt=tt: e.dma_start(
                                        out=out_d[tt * 128:(tt + 1) * 128,
                                                  jsl * 512:(jsl + 1) * 512],
                                        in_=fs[:, :]))
                        last_st_stage[tt % 2] = g_st
                return

            ffn_out()

            if _DEBUG:
                f.waitpair("gp", f.stpair())
                for nm, src in [("dbg_qsp", qsp_d[:, :]), ("dbg_kg", kg_d[:, :]),
                                ("dbg_vg", vg_d[:, :]), ("dbg_r1", r1_d[:, :])]:
                    f.do("gp", "st0", 16,
                         lambda e, nm=nm, src=src: e.dma_start(out=dbg[nm][:, :], in_=src))

            f.waitpair("gp", f.stpair())
            f.waitpair("sy", f.stpair())

        engmap = {"sy": "sync", "gp": "gpsimd", "sc": "scalar",
                  "ve": "vector", "te": "tensor"}
        for key, attr in engmap.items():
            def body(eng, key=key):
                flow(Fl(key, eng, sems))
            getattr(blk, attr)(body)

    return nc


_NC_CACHE = {}


def kernel(L0_attn_hidden_states, L0_attn_wq, L0_attn_wk, L0_attn_wv, L0_attn_wo,
           L0_ffn_k0, L0_ffn_s0, L0_ffn_k1, L0_ffn_s1, L0_ffn_k2, L0_ffn_s2):
    x = np.asarray(L0_attn_hidden_states)
    wqT = np.ascontiguousarray(np.asarray(L0_attn_wq).T)
    wkT = np.ascontiguousarray(np.asarray(L0_attn_wk).T)
    wvT = np.ascontiguousarray(np.asarray(L0_attn_wv).T)
    woT = np.ascontiguousarray(np.asarray(L0_attn_wo).T)
    k0 = np.ascontiguousarray(np.asarray(L0_ffn_k0)).view(np.uint8)
    k1 = np.ascontiguousarray(np.asarray(L0_ffn_k1)).view(np.uint8)
    k2 = np.ascontiguousarray(np.asarray(L0_ffn_k2)).view(np.uint8)
    sc = np.ascontiguousarray(np.broadcast_to(
        np.stack([np.float32(L0_ffn_s0), np.float32(L0_ffn_s1),
                  np.float32(L0_ffn_s2)]).astype(np.float32)[None, :], (128, 3)))

    if "nc" not in _NC_CACHE:
        _NC_CACHE["nc"] = build_program()
    nc = _NC_CACHE["nc"]

    in_maps = []
    for c in range(NCORE):
        seq, sub = c // 4, c % 4
        xs = np.concatenate([x[seq, k * 512 + sub * 128: k * 512 + sub * 128 + 128]
                             for k in range(NT)], axis=0)
        cos, sin = _host_rope(sub)
        in_maps.append({
            "x": np.ascontiguousarray(xs),
            "wqT": wqT, "wkT": wkT, "wvT": wvT, "woT": woT,
            "k0": k0, "k1": k1, "k2": k2, "sc": sc,
            "ropec": np.ascontiguousarray(cos), "ropes": np.ascontiguousarray(sin),
            "maskt": _host_mask(sub),
        })

    res = run_bass_kernel_spmd(nc, in_maps, core_ids=list(range(NCORE)))
    out = np.empty((B, S, D), dtype=np.float32)
    for c in range(NCORE):
        seq, sub = c // 4, c % 4
        oc = res.results[c]["out"]
        for k in range(NT):
            out[seq, k * 512 + sub * 128: k * 512 + sub * 128 + 128] = \
                oc[k * 128:(k + 1) * 128]
    if _DEBUG:
        kernel._dbg = res.results
    return out


# version 02
# speedup vs baseline: 1.1875x; 1.1875x over previous
import numpy as np
import ml_dtypes
from contextlib import ExitStack

import concourse.bass as bass
import concourse.mybir as mybir
from concourse.bass_utils import run_bass_kernel_spmd

F32 = mybir.dt.float32
BF16 = mybir.dt.bfloat16
U8 = mybir.dt.uint8
AF = mybir.ActivationFunctionType
ALU = mybir.AluOpType
AX = mybir.AxisListType
BF = ml_dtypes.bfloat16

B, S, D = 2, 4096, 4096
NH, HD, RDIM = 16, 256, 64
NE = 3277
TPC = 1024
NT = 8
NCORE = 8
MASKVAL = np.float32(-1.0e9 / 16.0)
INVD = 1.0 / 4096.0
EPS = 1e-5

_DEBUG = False
_STAGE = 5


def _host_tables():
    ident = np.eye(128, dtype=BF)
    prot = np.zeros((128, 128), dtype=np.float32)
    for j in range(32):
        prot[2 * j + 1, 2 * j] = -1.0
        prot[2 * j, 2 * j + 1] = 1.0
    prot = prot.astype(BF)
    cvec = np.empty((128, 32), dtype=np.float32)
    for ic in range(32):
        cvec[:, ic] = np.where(ic * 128 + np.arange(128) < NE, 2.0, -8.0)
    return ident, prot, cvec


def _host_rope(sub):
    pos = np.empty(TPC, dtype=np.float64)
    for k in range(NT):
        pos[k * 128:(k + 1) * 128] = k * 512 + sub * 128 + np.arange(128)
    j = np.arange(32, dtype=np.float64)
    inv_freq = 1.0 / (10000.0 ** (2.0 * j / RDIM))
    ang = inv_freq[:, None] * pos[None, :]
    sin = np.repeat(np.sin(ang), 2, axis=0)
    cos = np.repeat(np.cos(ang), 2, axis=0)
    return cos.astype(BF), sin.astype(BF)


def _host_mask(sub):
    m = np.zeros((128, 512), dtype=np.float32)
    qi = np.arange(128)[:, None]
    col = np.arange(512)[None, :]
    m[col > sub * 128 + qi] = MASKVAL
    return m


class Fl:

    def __init__(self, key, eng, sems):
        self.key = key
        self.eng = eng
        self.sems = sems
        self.c = {k: 0 for k in sems}

    def do(self, key, sem, amt, fn):
        self.c[sem] += amt
        if self.key == key:
            fn(self.eng).then_inc(self.sems[sem], amt)
        return self.c[sem]

    def nop(self, key, fn):
        if self.key == key:
            fn(self.eng)

    def wait(self, key, sem, val):
        if val is None or val <= 0:
            return
        if self.key == key:
            self.eng.wait_ge(self.sems[sem], val)

    def waitpair(self, key, pair):
        if pair is None:
            return
        (a, b) = pair
        self.wait(key, "st0", a)
        self.wait(key, "st1", b)

    def stpair(self):
        return (self.c["st0"], self.c["st1"])

    def cur(self, sem):
        return self.c[sem]

    def sop(self, key, fn):
        self.wait(key, key, self.c[key])
        return self.do(key, key, 1, fn)


def build_program():
    nc = bass.Bass("TRN2", target_bir_lowering=False, num_devices=8)

    x_d = nc.dram_tensor("x", [TPC, D], F32, kind="ExternalInput")
    wqT_d = nc.dram_tensor("wqT", [D, D], F32, kind="ExternalInput")
    wkT_d = nc.dram_tensor("wkT", [D, D], F32, kind="ExternalInput")
    wvT_d = nc.dram_tensor("wvT", [1024, 16384], F32, kind="ExternalInput")
    woT_d = nc.dram_tensor("woT", [1024, 16384], F32, kind="ExternalInput")
    k0_d = nc.dram_tensor("k0", [1024, 16384], U8, kind="ExternalInput")
    k1_d = nc.dram_tensor("k1", [1024, 16384], U8, kind="ExternalInput")
    k2_d = nc.dram_tensor("k2", [1024, 16384], U8, kind="ExternalInput")
    sc_d = nc.dram_tensor("sc", [128, 3], F32, kind="ExternalInput")
    ropec_d = nc.dram_tensor("ropec", [64, TPC], BF16, kind="ExternalInput")
    ropes_d = nc.dram_tensor("ropes", [64, TPC], BF16, kind="ExternalInput")
    mask_d = nc.dram_tensor("maskt", [128, 512], F32, kind="ExternalInput")
    out_d = nc.dram_tensor("out", [TPC, D], F32, kind="ExternalOutput")

    ident_np, prot_np, cvec_np = _host_tables()
    ident_d = nc.inline_tensor(ident_np, "ident")
    prot_d = nc.inline_tensor(prot_np, "prot")
    cvec_d = nc.inline_tensor(cvec_np, "cvec")

    qsp_d = nc.dram_tensor("qsp", [D, TPC], BF16, kind="Internal")
    kc_d = nc.dram_tensor("kcon", [D, TPC], BF16, kind="Internal")
    vc_d = nc.dram_tensor("vcon", [TPC, D], BF16, kind="Internal")
    kg_d = nc.dram_tensor("kgat", [4 * D, TPC], BF16, kind="Internal")
    vg_d = nc.dram_tensor("vgat", [4 * TPC, D], BF16, kind="Internal")
    r1_d = nc.dram_tensor("r1d", [TPC, D], F32, kind="Internal")

    dbg = {}
    if _DEBUG:
        for nm, shp, dt in [
            ("dbg_arena1", [128, 32768], BF16), ("dbg_qsp", [D, TPC], BF16),
            ("dbg_kg", [4 * D, TPC], BF16), ("dbg_vg", [4 * TPC, D], BF16),
            ("dbg_r1", [TPC, D], F32),
        ]:
            dbg[nm] = nc.dram_tensor(nm, shp, dt, kind="ExternalOutput")

    groups = [[0, 1, 2, 3], [4, 5, 6, 7]]

    with ExitStack() as ctx:
        E = ctx.enter_context
        A1 = E(nc.sbuf_tensor("arena1", [128, 65536], BF16))
        A2 = E(nc.sbuf_tensor("arena2", [128, 32768], BF16))
        cols = E(nc.sbuf_tensor("cols", [128, 16], F32))
        scb = E(nc.sbuf_tensor("scb", [128, 3], F32))
        ds0 = E(nc.sbuf_tensor("ds0", [128, 2], F32))
        cvec = E(nc.sbuf_tensor("cvec_sb", [128, 32], F32))
        cs1a = E(nc.sbuf_tensor("cs1a", [128, 32], F32))
        cs1b = E(nc.sbuf_tensor("cs1b", [128, 32], F32))
        cs2a = E(nc.sbuf_tensor("cs2a", [128, 32], F32))
        cs2b = E(nc.sbuf_tensor("cs2b", [128, 32], F32))
        ident = E(nc.sbuf_tensor("ident_sb", [128, 128], BF16))
        prot = E(nc.sbuf_tensor("prot_sb", [128, 128], BF16))
        maskt = E(nc.sbuf_tensor("mask_sb", [128, 512], F32))
        ropec = E(nc.sbuf_tensor("ropec_sb", [64, TPC], BF16))
        ropes = E(nc.sbuf_tensor("ropes_sb", [64, TPC], BF16))
        xsl = [E(nc.sbuf_tensor("xslA", [128, 512], F32)),
               E(nc.sbuf_tensor("xslB", [128, 512], F32))]
        fst = [E(nc.sbuf_tensor("fstA", [128, 512], F32)),
               E(nc.sbuf_tensor("fstB", [128, 512], F32))]
        kconst = E(nc.sbuf_tensor("kconst", [128, 2], F32))

        XF = A1[:, 0:32768]
        ATTN = A1[:, 0:32768]
        XF2 = A1[:, 0:32768]
        H2 = A1[:, 0:32768]
        WSs = [A1[:, 32768:36864], A1[:, 36864:40960]]
        WSb = [A1[:, 32768:49152], A1[:, 49152:65536]]
        KH = [A1[:, 32768:40960], A1[:, 40960:49152]]
        VH = [A1[:, 49152:57344], A1[:, 57344:65536]]

        a2f32 = A2.bitcast(F32)
        xt = [a2f32[:, 0:4096], a2f32[:, 4096:8192]]
        SQ = a2f32[:, 8192:12288]
        Sbuf = a2f32[:, 0:4096]
        Abuf = A2[:, 16384:20480]
        QT = [A2[:, 20480:20736], A2[:, 20736:20992]]
        ATsb = [A2[:, 20992:21504], A2[:, 21504:22016]]
        attn_st = A2[:, 22016:22272]
        xlnbf = A2[:, 24576:28672]
        t64a = a2f32[:, 14336:14848]
        t64b = a2f32[:, 14848:15360]
        rotbf = A2[:, 30720:31232]
        stbf = [A2[:, 31232:31744], A2[:, 31744:32256]]
        H1 = A2[:, 0:32768]

        pbA = E(nc.psum_tensor("pbA", [128, 512], F32))
        pbB = E(nc.psum_tensor("pbB", [128, 512], F32))
        pbC = E(nc.psum_tensor("pbC", [128, 512], F32))
        pbE = E(nc.psum_tensor("pbE", [128, 1024], BF16))
        pbF = E(nc.psum_tensor("pbF", [128, 1024], BF16))
        pmm = [pbA, pbB]
        ptp = [pbE, pbF]

        semnames = ["sdc", "xl0", "xl1", "wl0", "wl1", "kv0", "kv1",
                    "qt0", "qt1", "sl0", "sl1", "st0", "st1",
                    "te", "ve", "sc", "co"]
        sems = {n: E(nc.semaphore(n)) for n in semnames}

        blk = E(nc.Block())

        def flow(f: Fl):
            for dst, src in [
                (ident[:, :], ident_d[:, :]), (prot[:, :], prot_d[:, :]),
                (cvec[:, :], cvec_d[:, :]), (maskt[:, :], mask_d[:, :]),
                (ropec[:, :], ropec_d[:, :]), (ropes[:, :], ropes_d[:, :]),
                (scb[:, :], sc_d[:, :]),
            ]:
                f.do("sy", "sdc", 16, lambda e, d=dst, s=src: e.dma_start(out=d, in_=s))
            sd_const = f.cur("sdc")
            for k_ in ("te", "sc", "gp", "ve"):
                f.wait(k_, "sdc", sd_const)
            f.sop("ve", lambda e: e.memset(kconst[:, 0:1], EPS))
            f.sop("ve", lambda e: e.memset(kconst[:, 1:2], 0.0625))
            for (o, i, s1) in [(ds0[:, 0:1], scb[:, 0:1], 2.0),
                               (ds0[:, 1:2], scb[:, 0:1], -1.0)]:
                f.sop("ve", lambda e, o=o, i=i, s1=s1: e.tensor_scalar(
                    out=o, in0=i, scalar1=s1, scalar2=None, op0=ALU.mult))
            for (o, sidx, s2) in [(cs1a, 1, 2.0), (cs1b, 1, -1.0),
                                  (cs2a, 2, 2.0), (cs2b, 2, -1.0)]:
                f.sop("ve", lambda e, o=o, sidx=sidx, s2=s2: e.tensor_scalar(
                    out=o[:, :], in0=cvec[:, :], scalar1=scb[:, sidx:sidx + 1],
                    scalar2=s2, op0=ALU.mult, op1=ALU.mult))
            v_const = f.sop("ve",
                           lambda e: e.tensor_copy(out=cols[:, 15:16], in_=scb[:, 0:1]))

            def ln_pass(src_dram, dstFM, stgate):
                loads = []
                lastev_tp = [None, None]
                prev_xln = None
                for t in range(2):
                    f.waitpair("sy", stgate)
                    if t == 0:
                        f.wait("sy", "ve", v_const)
                    loads.append(f.do("sy", "xl%d" % (t % 2), 16,
                                      lambda e, t=t: e.dma_start(
                                          out=xt[t % 2][:, :],
                                          in_=src_dram[t * 128:(t + 1) * 128, :])))
                prev_sqred = None
                for t in range(NT):
                    xtile = xt[t % 2]
                    xls = "xl%d" % (t % 2)
                    f.wait("sc", xls, loads[t])
                    f.wait("sc", "ve", prev_sqred)
                    s_sq = f.sop("sc", lambda e, xtile=xtile: e.activation(
                        out=SQ[:, :], in_=xtile[:, :], func=AF.Square))
                    f.wait("ve", xls, loads[t])
                    f.sop("ve", lambda e, xtile=xtile: e.reduce_sum(
                        out=cols[:, 1:2], in_=xtile[:, :], axis=AX.X))
                    f.wait("ve", "sc", s_sq)
                    prev_sqred = f.sop("ve", lambda e: e.reduce_sum(
                        out=cols[:, 0:1], in_=SQ[:, :], axis=AX.X))
                    f.sop("ve", lambda e: e.tensor_scalar(
                        out=cols[:, 2:3], in0=cols[:, 1:2], scalar1=INVD,
                        scalar2=None, op0=ALU.mult))
                    f.sop("ve", lambda e: e.tensor_tensor(
                        out=cols[:, 3:4], in0=cols[:, 2:3], in1=cols[:, 2:3],
                        op=ALU.mult))
                    v_var = f.sop("ve", lambda e: e.scalar_tensor_tensor(
                        out=cols[:, 4:5], in0=cols[:, 0:1], scalar=INVD,
                        in1=cols[:, 3:4], op0=ALU.mult, op1=ALU.subtract))
                    f.wait("sc", "ve", v_var)
                    s_std = f.sop("sc",
                                 lambda e: e.activation(out=cols[:, 5:6], in_=cols[:, 4:5],
                                                        func=AF.Sqrt, bias=kconst[:, 0:1]))
                    f.wait("ve", "sc", s_std)
                    f.sop("ve", lambda e: e.reciprocal(out=cols[:, 6:7], in_=cols[:, 5:6]))
                    v_nmr = f.sop("ve", lambda e: e.tensor_scalar(
                        out=cols[:, 7:8], in0=cols[:, 2:3], scalar1=cols[:, 6:7],
                        scalar2=-1.0, op0=ALU.mult, op1=ALU.mult))
                    f.wait("sc", "ve", v_nmr)
                    f.wait("sc", "te", prev_xln)
                    v_xln = f.sop("sc",
                                 lambda e, xtile=xtile: e.activation(
                                     out=xlnbf[:, :], in_=xtile[:, :], func=AF.Identity,
                                     scale=cols[:, 6:7], bias=cols[:, 7:8]))
                    if t + 2 < NT:
                        f.wait("sy", "sc", v_xln)
                        loads.append(f.do("sy", xls, 16,
                                          lambda e, t=t: e.dma_start(
                                              out=xt[t % 2][:, :],
                                              in_=src_dram[(t + 2) * 128:(t + 3) * 128, :])))
                    f.wait("te", "sc", v_xln)
                    t_last = None
                    for g in range(8):
                        bank = ptp[g % 2]
                        f.wait("te", "ve", lastev_tp[g % 2])
                        def tgrp(e, g=g, bank=bank):
                            for j in range(4):
                                dc = g * 4 + j
                                i = e.transpose(bank[:, j * 128:(j + 1) * 128],
                                                xlnbf[:, dc * 128:(dc + 1) * 128],
                                                ident[:, :])
                            return i
                        t_grp = f.do("te", "te", 1, tgrp)
                        t_last = t_grp
                        f.wait("ve", "te", t_grp)
                        def evac(e, g=g, bank=bank, t=t):
                            dst = dstFM.rearrange("p (dc tt) -> p dc tt", dc=32)[
                                :, g * 4:(g + 1) * 4, t * 128:(t + 1) * 128]
                            return e.tensor_copy(
                                out=dst, in_=bank[:, 0:512].rearrange(
                                    "p (j c) -> p j c", j=4))
                        lastev_tp[g % 2] = f.sop("ve", evac)
                    prev_xln = t_last
                return f.cur("ve")

            v_xf_done = ln_pass(x_d, XF, None)

            if _STAGE <= 1:
                if _DEBUG:
                    f.wait("gp", "ve", v_xf_done)
                    f.do("gp", "st0", 16, lambda e: e.dma_start(
                        out=dbg["dbg_arena1"][:, :], in_=A1[:, 0:32768]))
                f.waitpair("gp", f.stpair())
                f.waitpair("sy", f.stpair())
                return

            def fm_proj(wT_d, dst_dram, do_rope, do_scale, v_in_ready):
                te0 = f.cur("te")
                last_mm_slab = [None, None]
                last_ev_bank = [None, None]
                last_st_stage = [f.cur("st0"), f.cur("st1")]
                last_v_rot = None
                for ms in range(32):
                    buf = WSs[ms % 2]
                    wls = "wl%d" % (ms % 2)
                    f.wait("gp", "te", last_mm_slab[ms % 2] or te0)
                    src = wT_d[ms * 128:(ms + 1) * 128, :]
                    gl = f.do("gp", wls, 16,
                              lambda e, buf=buf, src=src: e.dma_start(out=buf[:, :], in_=src))
                    for ts2 in range(2):
                        bank = pmm[ts2]
                        sts = "st%d" % ts2
                        f.wait("te", wls, gl)
                        if ms == 0 and ts2 == 0:
                            f.wait("te", "ve", v_in_ready)
                        f.wait("te", "sc", last_ev_bank[ts2])
                        def mm(e, buf=buf, bank=bank, ts2=ts2):
                            for dc in range(32):
                                i = e.matmul(
                                    bank[:, :], buf[:, dc * 128:(dc + 1) * 128],
                                    XF[:, dc * 1024 + ts2 * 512: dc * 1024 + ts2 * 512 + 512],
                                    start=(dc == 0), stop=(dc == 31))
                            return i
                        t_mm = f.do("te", "te", 1, mm)
                        last_mm_slab[ms % 2] = t_mm
                        st = stbf[ts2]
                        f.wait("sc", sts, last_st_stage[ts2])
                        f.wait("sc", "te", t_mm)
                        s_ev = f.sop("sc",
                                    lambda e, st=st, bank=bank, do_scale=do_scale: e.activation(
                                        out=st[:, :], in_=bank[:, :], func=AF.Copy,
                                        scale=(kconst[:, 1:2] if do_scale else 1.0)))
                        last_ev_bank[ts2] = s_ev
                        prod_sem, prod_val = "sc", s_ev
                        if do_rope and (ms % 2 == 0):
                            f.wait("te", "sc", s_ev)
                            f.wait("te", "ve", last_v_rot)
                            t_rot = f.do("te", "te", 1,
                                         lambda e, st=st: e.matmul(
                                             pbC[:, :], prot[:, :], st[:, :],
                                             start=True, stop=True))
                            f.wait("sc", "te", t_rot)
                            s_rb = f.sop("sc",
                                        lambda e: e.activation(out=rotbf[0:64, :],
                                                               in_=pbC[0:64, :],
                                                               func=AF.Copy))
                            last_ev_bank[ts2] = s_rb
                            f.wait("ve", "sc", s_ev)
                            f.sop("ve", lambda e, st=st, ts2=ts2: e.tensor_tensor(
                                out=t64a[0:64, :], in0=st[0:64, :],
                                in1=ropec[:, ts2 * 512:(ts2 + 1) * 512], op=ALU.mult))
                            f.wait("ve", "sc", s_rb)
                            f.sop("ve", lambda e, ts2=ts2: e.tensor_tensor(
                                out=t64b[0:64, :], in0=rotbf[0:64, :],
                                in1=ropes[:, ts2 * 512:(ts2 + 1) * 512], op=ALU.mult))
                            v_rot = f.sop("ve",
                                         lambda e, st=st: e.tensor_tensor(
                                             out=st[0:64, :], in0=t64a[0:64, :],
                                             in1=t64b[0:64, :], op=ALU.add))
                            last_v_rot = v_rot
                            prod_sem, prod_val = "ve", v_rot
                        f.wait("gp", prod_sem, prod_val)
                        g_st = f.do("gp", sts, 16,
                                    lambda e, st=st, ms=ms, ts2=ts2, dd=dst_dram: e.dma_start(
                                        out=dd[ms * 128:(ms + 1) * 128,
                                               ts2 * 512:(ts2 + 1) * 512],
                                        in_=st[:, :]))
                        last_st_stage[ts2] = g_st
                return f.stpair()

            kp = fm_proj(wkT_d, kc_d, do_rope=True, do_scale=False,
                         v_in_ready=v_xf_done)
            f.waitpair("gp", kp)
            co_k = f.do("gp", "co", 1,
                        lambda e: e.collective_compute(
                            "AllGather", ALU.bypass, replica_groups=groups,
                            ins=[kc_d.ap().opt()], outs=[kg_d.ap().opt()]))

            def tm_proj(wT_d, dst_dram, src_act, r1_add, te_ready, ve_ready):
                te0 = f.cur("te")
                last_mm_slab = [None, None]
                last_ev_bank = [None, None]
                last_st_stage = [f.cur("st0"), f.cur("st1")]
                last_sl_ev = [None, None]
                evsem = "ve" if r1_add is not None else "sc"
                for ms in range(8):
                    buf = WSb[ms % 2]
                    wls = "wl%d" % (ms % 2)
                    f.wait("gp", "te", last_mm_slab[ms % 2] or te0)
                    src = wT_d[ms * 128:(ms + 1) * 128, :]
                    gl = f.do("gp", wls, 16,
                              lambda e, buf=buf, src=src: e.dma_start(out=buf[:, :], in_=src))
                    for tt in range(NT):
                        bank = pmm[tt % 2]
                        sts = "st%d" % (tt % 2)
                        sls = "sl%d" % (tt % 2)
                        sl = None
                        if r1_add is not None:
                            xs = xsl[tt % 2]
                            f.wait("sy", "ve", last_sl_ev[tt % 2])
                            sl = f.do("sy", sls, 16,
                                      lambda e, xs=xs, ms=ms, tt=tt: e.dma_start(
                                          out=xs[:, :],
                                          in_=r1_add[tt * 128:(tt + 1) * 128,
                                                     ms * 512:(ms + 1) * 512]))
                        f.wait("te", wls, gl)
                        if ms == 0 and tt == 0:
                            f.wait("te", "ve", ve_ready)
                            f.wait("te", "te", te_ready)
                        f.wait("te", evsem, last_ev_bank[tt % 2])
                        def mmv(e, buf=buf, bank=bank, tt=tt):
                            for dc in range(32):
                                i = e.matmul(
                                    bank[:, :],
                                    src_act[:, dc * 1024 + tt * 128: dc * 1024 + tt * 128 + 128],
                                    buf[:, dc * 512:(dc + 1) * 512],
                                    start=(dc == 0), stop=(dc == 31))
                            return i
                        t_mm = f.do("te", "te", 1, mmv)
                        last_mm_slab[ms % 2] = t_mm
                        if r1_add is None:
                            st = stbf[tt % 2]
                            f.wait("sc", sts, last_st_stage[tt % 2])
                            f.wait("sc", "te", t_mm)
                            s_ev = f.sop("sc",
                                        lambda e, st=st, bank=bank: e.activation(
                                            out=st[:, :], in_=bank[:, :], func=AF.Copy))
                            last_ev_bank[tt % 2] = s_ev
                            f.wait("gp", "sc", s_ev)
                            g_st = f.do("gp", sts, 16,
                                        lambda e, st=st, ms=ms, tt=tt: e.dma_start(
                                            out=dst_dram[tt * 128:(tt + 1) * 128,
                                                         ms * 512:(ms + 1) * 512],
                                            in_=st[:, :]))
                        else:
                            fs = fst[tt % 2]
                            f.wait("ve", sts, last_st_stage[tt % 2])
                            f.wait("ve", "te", t_mm)
                            f.wait("ve", sls, sl)
                            v_ev = f.sop("ve",
                                        lambda e, bank=bank, xs=xs, fs=fs: e.scalar_tensor_tensor(
                                            out=fs[:, :], in0=bank[:, :], scalar=1.0,
                                            in1=xs[:, :], op0=ALU.mult, op1=ALU.add))
                            last_ev_bank[tt % 2] = v_ev
                            last_sl_ev[tt % 2] = v_ev
                            f.wait("gp", "ve", v_ev)
                            g_st = f.do("gp", sts, 16,
                                        lambda e, fs=fs, ms=ms, tt=tt: e.dma_start(
                                            out=dst_dram[tt * 128:(tt + 1) * 128,
                                                         ms * 512:(ms + 1) * 512],
                                            in_=fs[:, :]))
                        last_st_stage[tt % 2] = g_st
                return f.stpair()

            vp = tm_proj(wvT_d, vc_d, XF, None, 0, v_xf_done)
            f.waitpair("gp", vp)
            co_v = f.do("gp", "co", 1,
                        lambda e: e.collective_compute(
                            "AllGather", ALU.bypass, replica_groups=groups,
                            ins=[vc_d.ap().opt()], outs=[vg_d.ap().opt()]))

            qp = fm_proj(wqT_d, qsp_d, do_rope=True, do_scale=True,
                         v_in_ready=v_xf_done)
            te_q_done = f.cur("te")

            if _STAGE <= 2:
                if _DEBUG:
                    f.waitpair("gp", f.stpair())
                    f.wait("gp", "co", f.cur("co"))
                    f.wait("gp", "te", te_q_done)
                    for nm, src in [("dbg_arena1", A1[:, 0:32768]),
                                    ("dbg_qsp", qsp_d[:, :]),
                                    ("dbg_kg", kg_d[:, :]), ("dbg_vg", vg_d[:, :])]:
                        f.do("gp", "st0", 16,
                             lambda e, nm=nm, src=src: e.dma_start(out=dbg[nm][:, :], in_=src))
                f.waitpair("gp", f.stpair())
                f.waitpair("sy", f.stpair())
                return

            kv_loaded = {}
            kv_users = {}

            def load_head(h):
                par = h % 2
                kvs = "kv%d" % par
                f.wait("sy", "te", kv_users.get(h - 2))
                if h <= 1:
                    f.wait("sy", "co", co_v)
                    f.wait("sy", "te", te_q_done)
                    f.waitpair("sy", qp)
                last = None
                for dc in range(2):
                    for sb in range(4):
                        srcrow = sb * 4096 + h * 256 + dc * 128
                        dst = KH[par][:, dc * 4096 + sb * 1024: dc * 4096 + (sb + 1) * 1024]
                        last = f.do("sy", kvs, 16,
                                    lambda e, d=dst, srcrow=srcrow: e.dma_start(
                                        out=d, in_=kg_d[srcrow:srcrow + 128, :]))
                for sb in range(4):
                    src = vg_d[sb * 1024:(sb + 1) * 1024, h * 256:(h + 1) * 256].rearrange(
                        "(kc p) c -> p kc c", p=128)
                    dst = VH[par].rearrange("p (kc four c) -> p kc four c",
                                            four=4, c=256)[:, :, sb, :]
                    last = f.do("sy", kvs, 16,
                                lambda e, d=dst, s=src: e.dma_start(out=d, in_=s))
                kv_loaded[h] = last

            load_head(0)
            load_head(1)
            last_qt_user = [None, None]
            last_sc_ev = [None, None]
            last_at_ev = [None, None]
            prev_s_exp = None
            prev_t_at_last = None
            prev_v_attn = None
            for h in range(NH):
                par = h % 2
                kvs = "kv%d" % par
                t_av = None
                for k in range(NT):
                    qpar = k % 2
                    qts = "qt%d" % qpar
                    f.wait("sy", "te", last_qt_user[qpar])
                    ql = None
                    for dc in range(2):
                        ql = f.do("sy", qts, 16,
                                  lambda e, h=h, k=k, qpar=qpar, dc=dc: e.dma_start(
                                      out=QT[qpar][:, dc * 128:(dc + 1) * 128],
                                      in_=qsp_d[h * 256 + dc * 128: h * 256 + dc * 128 + 128,
                                                k * 128:(k + 1) * 128]))
                    for kc in range(k + 1):
                        bank = pmm[kc % 2]
                        f.wait("te", "ve", last_sc_ev[kc % 2])
                        f.wait("te", qts, ql)
                        if kc == 0:
                            f.wait("te", kvs, kv_loaded[h])
                        def smm(e, bank=bank, kc=kc, qpar=qpar, par=par):
                            for dc in range(2):
                                rhs = KH[par][:, dc * 4096:(dc + 1) * 4096].rearrange(
                                    "p (sb c) -> p sb c", sb=4)[:, :, kc * 128:(kc + 1) * 128]
                                i = e.matmul(
                                    bank[:, :], QT[qpar][:, dc * 128:(dc + 1) * 128],
                                    rhs, start=(dc == 0), stop=(dc == 1))
                            return i
                        t_s = f.do("te", "te", 1, smm)
                        f.wait("ve", "te", t_s)
                        if kc == 0:
                            f.wait("ve", "sc", prev_s_exp)
                        if kc == k:
                            v_ev = f.sop("ve",
                                        lambda e, bank=bank, kc=kc: e.tensor_tensor(
                                            out=Sbuf[:, kc * 512:(kc + 1) * 512],
                                            in0=bank[:, :], in1=maskt[:, :], op=ALU.add))
                        else:
                            v_ev = f.sop("ve",
                                        lambda e, bank=bank, kc=kc: e.tensor_copy(
                                            out=Sbuf[:, kc * 512:(kc + 1) * 512],
                                            in_=bank[:, :]))
                        last_sc_ev[kc % 2] = v_ev
                    L = (k + 1) * 512
                    f.sop("ve", lambda e, L=L: e.reduce_max(
                        out=cols[:, 8:9], in_=Sbuf[:, 0:L],
                        axis=AX.X, negate=True))
                    v_m = f.sop("ve",
                               lambda e: e.tensor_copy(out=cols[:, 11:12],
                                                       in_=cols[:, 8:9]))
                    f.wait("sc", "ve", v_m)
                    f.wait("sc", "te", prev_t_at_last)
                    s_exp = f.sop("sc",
                                 lambda e, L=L: e.activation(
                                     out=Abuf[:, 0:L], in_=Sbuf[:, 0:L], func=AF.Exp,
                                     bias=cols[:, 8:9]))
                    prev_s_exp = s_exp
                    f.wait("ve", "sc", s_exp)
                    f.sop("ve", lambda e, L=L: e.reduce_sum(
                        out=cols[:, 9:10], in_=Abuf[:, 0:L], axis=AX.X))
                    v_rinv = f.sop("ve",
                                  lambda e: e.reciprocal(out=cols[:, 10:11],
                                                         in_=cols[:, 9:10]))
                    t_at = None
                    atev_of = {}

                    def emit_tat(kc):
                        tb = ptp[kc % 2]
                        f.wait("te", "ve", last_at_ev[kc % 2])
                        if kc == 0:
                            f.wait("te", "sc", s_exp)
                            f.wait("te", "ve", prev_v_attn)
                        def tat(e, tb=tb, kc=kc):
                            for j in range(4):
                                c0 = kc * 512 + j * 128
                                i = e.transpose(tb[:, j * 128:(j + 1) * 128],
                                                Abuf[:, c0:c0 + 128], ident[:, :])
                            return i
                        t_at = f.do("te", "te", 1, tat)
                        f.wait("ve", "te", t_at)
                        v_atev = f.sop("ve",
                                      lambda e, tb=tb, kc=kc: e.tensor_copy(
                                          out=ATsb[kc % 2][:, :], in_=tb[:, 0:512]))
                        last_at_ev[kc % 2] = v_atev
                        atev_of[kc] = v_atev
                        return t_at

                    def emit_avm(kc):
                        f.wait("te", "ve", atev_of[kc])
                        def avm(e, kc=kc, par=par, k=k):
                            for j in range(4):
                                piece = kc * 4 + j
                                i = e.matmul(
                                    pbC[:, 0:256], ATsb[kc % 2][:, j * 128:(j + 1) * 128],
                                    VH[par][:, piece * 256:(piece + 1) * 256],
                                    start=(kc == 0 and j == 0),
                                    stop=(kc == k and j == 3))
                            return i
                        return f.do("te", "te", 1, avm)

                    for kc in range(k + 1):
                        t_at = emit_tat(kc)
                        if kc >= 1:
                            t_av = emit_avm(kc - 1)
                    t_av = emit_avm(k)
                    prev_t_at_last = t_at
                    last_qt_user[qpar] = t_av
                    f.wait("ve", "te", t_av)
                    v_at = f.sop("ve",
                                lambda e: e.tensor_scalar(
                                    out=attn_st[:, :], in0=pbC[:, 0:256],
                                    scalar1=cols[:, 10:11], scalar2=None, op0=ALU.mult))
                    f.wait("te", "ve", v_at)
                    def tattn(e):
                        for dh in range(2):
                            i = e.transpose(pbE[:, 512 + dh * 128: 512 + (dh + 1) * 128],
                                            attn_st[:, dh * 128:(dh + 1) * 128],
                                            ident[:, :])
                        return i
                    t_ta = f.do("te", "te", 1, tattn)
                    f.wait("ve", "te", t_ta)
                    def evat(e, h=h, k=k):
                        dst = ATTN.rearrange("p (hc t) -> p hc t", hc=32)[
                            :, 2 * h:2 * h + 2, k * 128:(k + 1) * 128]
                        return e.tensor_copy(out=dst, in_=pbE[:, 512:768].rearrange(
                            "p (dh c) -> p dh c", dh=2))
                    prev_v_attn = f.sop("ve", evat)
                kv_users[h] = t_av
                if h + 2 < NH:
                    load_head(h + 2)
            v_attn_done = f.cur("ve")
            te_attn_done = f.cur("te")

            r1p = tm_proj(woT_d, r1_d, ATTN, x_d, te_attn_done, v_attn_done)

            v_xf2_done = ln_pass(r1_d, XF2, r1p)

            if _STAGE <= 4:
                if _DEBUG:
                    f.waitpair("gp", f.stpair())
                    f.wait("gp", "ve", v_xf2_done)
                    for nm, src in [("dbg_qsp", qsp_d[:, :]), ("dbg_kg", kg_d[:, :]),
                                    ("dbg_vg", vg_d[:, :]), ("dbg_r1", r1_d[:, :]),
                                    ("dbg_arena1", A1[:, 0:32768])]:
                        f.do("gp", "st0", 16,
                             lambda e, nm=nm, src=src: e.dma_start(out=dbg[nm][:, :], in_=src))
                f.waitpair("gp", f.stpair())
                f.waitpair("sy", f.stpair())
                return

            def ffn_fm(kx_d, sa, sb_, src_act, dstH, sc_in_ready, ve_in_ready):
                te0 = f.cur("te")
                last_mm_slab = [None, None]
                last_ev_bank = [None, None]
                for jsl in range(8):
                    buf = WSb[jsl % 2]
                    wls = "wl%d" % (jsl % 2)
                    f.wait("gp", "te", last_mm_slab[jsl % 2] or te0)
                    src = kx_d[jsl * 128:(jsl + 1) * 128, :]
                    gl = f.do("gp", wls, 16,
                              lambda e, buf=buf, src=src: e.dma_start(out=buf[:, :], in_=src))
                    f.wait("ve", wls, gl)
                    f.wait("ve", "te", last_mm_slab[jsl % 2] or te0)
                    def conv(e, buf=buf):
                        for ic in range(32):
                            s1 = sa[:, ic:ic + 1] if sa.shape[1] == 32 else sa
                            s2 = sb_[:, ic:ic + 1] if sb_.shape[1] == 32 else sb_
                            i = e.tensor_scalar(out=buf[:, ic * 512:(ic + 1) * 512],
                                                in0=buf[:, ic * 512:(ic + 1) * 512],
                                                scalar1=s1, scalar2=s2,
                                                op0=ALU.mult, op1=ALU.add)
                        return i
                    v_cv = f.sop("ve", conv)
                    for j1 in range(4):
                        for ts2 in range(2):
                            bank = pmm[ts2]
                            f.wait("te", "ve", v_cv)
                            if jsl == 0 and j1 == 0 and ts2 == 0:
                                f.wait("te", "sc", sc_in_ready)
                                f.wait("te", "ve", ve_in_ready)
                            f.wait("te", "sc", last_ev_bank[ts2])
                            def mmf(e, buf=buf, bank=bank, j1=j1, ts2=ts2):
                                for ic in range(32):
                                    i = e.matmul(
                                        bank[:, :],
                                        buf[:, ic * 512 + j1 * 128: ic * 512 + j1 * 128 + 128],
                                        src_act[:, ic * 1024 + ts2 * 512: ic * 1024 + ts2 * 512 + 512],
                                        start=(ic == 0), stop=(ic == 31))
                                return i
                            t_mm = f.do("te", "te", 1, mmf)
                            last_mm_slab[jsl % 2] = t_mm
                            f.wait("sc", "te", t_mm)
                            jc = jsl * 4 + j1
                            s_ev = f.sop("sc",
                                        lambda e, bank=bank, jc=jc, ts2=ts2: e.activation(
                                            out=dstH[:, jc * 1024 + ts2 * 512:
                                                     jc * 1024 + ts2 * 512 + 512],
                                            in_=bank[:, :], func=AF.Relu))
                            last_ev_bank[ts2] = s_ev
                return f.cur("sc")

            sc_w0_done = ffn_fm(k0_d, ds0[:, 0:1], ds0[:, 1:2], XF2, H1,
                                f.cur("sc"), v_xf2_done)
            sc_w1_done = ffn_fm(k1_d, cs1a, cs1b, H1, H2, sc_w0_done, 0)

            def ffn_out():
                te0 = f.cur("te")
                last_mm_slab = [None, None]
                last_ev_bank = [None, None]
                last_st_stage = [f.cur("st0"), f.cur("st1")]
                last_sl_ev = [None, None]
                for jsl in range(8):
                    buf = WSb[jsl % 2]
                    wls = "wl%d" % (jsl % 2)
                    f.wait("gp", "te", last_mm_slab[jsl % 2] or te0)
                    src = k2_d[jsl * 128:(jsl + 1) * 128, :]
                    gl = f.do("gp", wls, 16,
                              lambda e, buf=buf, src=src: e.dma_start(out=buf[:, :], in_=src))
                    f.wait("ve", wls, gl)
                    f.wait("ve", "te", last_mm_slab[jsl % 2] or te0)
                    def conv(e, buf=buf):
                        for ic in range(32):
                            i = e.tensor_scalar(out=buf[:, ic * 512:(ic + 1) * 512],
                                                in0=buf[:, ic * 512:(ic + 1) * 512],
                                                scalar1=cs2a[:, ic:ic + 1],
                                                scalar2=cs2b[:, ic:ic + 1],
                                                op0=ALU.mult, op1=ALU.add)
                        return i
                    v_cv = f.sop("ve", conv)
                    for tt in range(NT):
                        bank = pmm[tt % 2]
                        sts = "st%d" % (tt % 2)
                        sls = "sl%d" % (tt % 2)
                        rs = xsl[tt % 2]
                        f.wait("sy", "ve", last_sl_ev[tt % 2])
                        if jsl == 0 and tt <= 1:
                            f.waitpair("sy", r1p)
                        sl = f.do("sy", sls, 16,
                                  lambda e, rs=rs, jsl=jsl, tt=tt: e.dma_start(
                                      out=rs[:, :],
                                      in_=r1_d[tt * 128:(tt + 1) * 128,
                                               jsl * 512:(jsl + 1) * 512]))
                        f.wait("te", "ve", v_cv)
                        if jsl == 0 and tt == 0:
                            f.wait("te", "sc", sc_w1_done)
                        f.wait("te", "ve", last_ev_bank[tt % 2])
                        def mmf2(e, buf=buf, bank=bank, tt=tt):
                            for ic in range(32):
                                i = e.matmul(
                                    bank[:, :],
                                    H2[:, ic * 1024 + tt * 128: ic * 1024 + tt * 128 + 128],
                                    buf[:, ic * 512:(ic + 1) * 512],
                                    start=(ic == 0), stop=(ic == 31))
                            return i
                        t_mm = f.do("te", "te", 1, mmf2)
                        last_mm_slab[jsl % 2] = t_mm
                        fs = fst[tt % 2]
                        f.wait("ve", sts, last_st_stage[tt % 2])
                        f.wait("ve", "te", t_mm)
                        f.wait("ve", sls, sl)
                        v_ev = f.sop("ve",
                                    lambda e, bank=bank, rs=rs, fs=fs: e.scalar_tensor_tensor(
                                        out=fs[:, :], in0=bank[:, :], scalar=1.0,
                                        in1=rs[:, :], op0=ALU.mult, op1=ALU.add))
                        last_ev_bank[tt % 2] = v_ev
                        last_sl_ev[tt % 2] = v_ev
                        f.wait("gp", "ve", v_ev)
                        g_st = f.do("gp", sts, 16,
                                    lambda e, fs=fs, jsl=jsl, tt=tt: e.dma_start(
                                        out=out_d[tt * 128:(tt + 1) * 128,
                                                  jsl * 512:(jsl + 1) * 512],
                                        in_=fs[:, :]))
                        last_st_stage[tt % 2] = g_st
                return

            ffn_out()

            if _DEBUG:
                f.waitpair("gp", f.stpair())
                for nm, src in [("dbg_qsp", qsp_d[:, :]), ("dbg_kg", kg_d[:, :]),
                                ("dbg_vg", vg_d[:, :]), ("dbg_r1", r1_d[:, :])]:
                    f.do("gp", "st0", 16,
                         lambda e, nm=nm, src=src: e.dma_start(out=dbg[nm][:, :], in_=src))

            f.waitpair("gp", f.stpair())
            f.waitpair("sy", f.stpair())

        engmap = {"sy": "sync", "gp": "gpsimd", "sc": "scalar",
                  "ve": "vector", "te": "tensor"}
        for key, attr in engmap.items():
            def body(eng, key=key):
                flow(Fl(key, eng, sems))
            getattr(blk, attr)(body)

    return nc


_NC_CACHE = {}


def kernel(L0_attn_hidden_states, L0_attn_wq, L0_attn_wk, L0_attn_wv, L0_attn_wo,
           L0_ffn_k0, L0_ffn_s0, L0_ffn_k1, L0_ffn_s1, L0_ffn_k2, L0_ffn_s2):
    x = np.asarray(L0_attn_hidden_states)

    def swz128(wT):
        return np.ascontiguousarray(
            wT.reshape(32, 128, 32, 128).transpose(2, 1, 0, 3).reshape(4096, 4096))

    def swz512(wT):
        return np.ascontiguousarray(
            wT.reshape(32, 128, 8, 512).transpose(2, 1, 0, 3).reshape(1024, 16384))

    wqT = swz128(np.asarray(L0_attn_wq).T)
    wkT = swz128(np.asarray(L0_attn_wk).T)
    wvT = swz512(np.asarray(L0_attn_wv).T)
    woT = swz512(np.asarray(L0_attn_wo).T)
    k0 = swz512(np.asarray(L0_ffn_k0).view(np.uint8))
    k1 = swz512(np.asarray(L0_ffn_k1).view(np.uint8))
    k2 = swz512(np.asarray(L0_ffn_k2).view(np.uint8))
    sc = np.ascontiguousarray(np.broadcast_to(
        np.stack([np.float32(L0_ffn_s0), np.float32(L0_ffn_s1),
                  np.float32(L0_ffn_s2)]).astype(np.float32)[None, :], (128, 3)))

    if "nc" not in _NC_CACHE:
        _NC_CACHE["nc"] = build_program()
    nc = _NC_CACHE["nc"]

    in_maps = []
    for c in range(NCORE):
        seq, sub = c // 4, c % 4
        xs = np.concatenate([x[seq, k * 512 + sub * 128: k * 512 + sub * 128 + 128]
                             for k in range(NT)], axis=0)
        cos, sin = _host_rope(sub)
        in_maps.append({
            "x": np.ascontiguousarray(xs),
            "wqT": wqT, "wkT": wkT, "wvT": wvT, "woT": woT,
            "k0": k0, "k1": k1, "k2": k2, "sc": sc,
            "ropec": np.ascontiguousarray(cos), "ropes": np.ascontiguousarray(sin),
            "maskt": _host_mask(sub),
        })

    res = run_bass_kernel_spmd(nc, in_maps, core_ids=list(range(NCORE)))
    out = np.empty((B, S, D), dtype=np.float32)
    for c in range(NCORE):
        seq, sub = c // 4, c % 4
        oc = res.results[c]["out"]
        for k in range(NT):
            out[seq, k * 512 + sub * 128: k * 512 + sub * 128 + 128] = \
                oc[k * 128:(k + 1) * 128]
    if _DEBUG:
        kernel._dbg = res.results
    return out


# version 04 (final)
# speedup vs baseline: 1.3980x; 1.1772x over previous
import numpy as np
import ml_dtypes
from contextlib import ExitStack

import concourse.bass as bass
import concourse.mybir as mybir
from concourse.bass_utils import run_bass_kernel_spmd

F32 = mybir.dt.float32
BF16 = mybir.dt.bfloat16
U8 = mybir.dt.uint8
AF = mybir.ActivationFunctionType
ALU = mybir.AluOpType
AX = mybir.AxisListType
BF = ml_dtypes.bfloat16

B, S, D = 2, 4096, 4096
NH, HD, RDIM = 16, 256, 64
NE = 3277
TPC = 1024
NT = 8
NCORE = 8
MASKVAL = np.float32(-1.0e9 / 16.0)
INVD = 1.0 / 4096.0
EPS = 1e-5

_DEBUG = False
_STAGE = 5


def _host_tables():
    ident = np.eye(128, dtype=BF)
    prot = np.zeros((128, 128), dtype=np.float32)
    for j in range(32):
        prot[2 * j + 1, 2 * j] = -1.0
        prot[2 * j, 2 * j + 1] = 1.0
    prot = prot.astype(BF)
    cvec = np.empty((128, 32), dtype=np.float32)
    for ic in range(32):
        cvec[:, ic] = np.where(ic * 128 + np.arange(128) < NE, 2.0, -8.0)
    return ident, prot, cvec


def _host_rope(sub):
    pos = np.empty(TPC, dtype=np.float64)
    for k in range(NT):
        pos[k * 128:(k + 1) * 128] = k * 512 + sub * 128 + np.arange(128)
    j = np.arange(32, dtype=np.float64)
    inv_freq = 1.0 / (10000.0 ** (2.0 * j / RDIM))
    ang = inv_freq[:, None] * pos[None, :]
    sin = np.repeat(np.sin(ang), 2, axis=0)
    cos = np.repeat(np.cos(ang), 2, axis=0)
    return cos.astype(BF), sin.astype(BF)


def _host_mask(sub):
    m = np.zeros((128, 512), dtype=np.float32)
    qi = np.arange(128)[:, None]
    col = np.arange(512)[None, :]
    m[col > sub * 128 + qi] = MASKVAL
    return m


class Fl:

    SOPSEM = {"ve": "ve", "sc": "sc", "gp": "po"}

    def __init__(self, key, eng, sems):
        self.key = key
        self.eng = eng
        self.sems = sems
        self.c = {k: 0 for k in sems}

    def do(self, key, sem, amt, fn):
        self.c[sem] += amt
        if self.key == key:
            fn(self.eng).then_inc(self.sems[sem], amt)
        return self.c[sem]

    def wait(self, key, sem, val):
        if val is None or val <= 0:
            return
        if self.key == key:
            self.eng.wait_ge(self.sems[sem], val)

    def waitpair(self, key, pair):
        if pair is None:
            return
        (a, b) = pair
        self.wait(key, "st0", a)
        self.wait(key, "st1", b)

    def stpair(self):
        return (self.c["st0"], self.c["st1"])

    def cur(self, sem):
        return self.c[sem]

    def sop(self, key, fn):
        sem = self.SOPSEM[key]
        self.wait(key, sem, self.c[sem])
        return self.do(key, sem, 1, fn)


def build_program():
    nc = bass.Bass("TRN2", target_bir_lowering=False, num_devices=8)

    x_d = nc.dram_tensor("x", [TPC, D], F32, kind="ExternalInput")
    wqT_d = nc.dram_tensor("wqT", [D, D], F32, kind="ExternalInput")
    wkT_d = nc.dram_tensor("wkT", [D, D], F32, kind="ExternalInput")
    wvT_d = nc.dram_tensor("wvT", [1024, 16384], F32, kind="ExternalInput")
    woT_d = nc.dram_tensor("woT", [1024, 16384], F32, kind="ExternalInput")
    k0_d = nc.dram_tensor("k0", [1024, 16384], U8, kind="ExternalInput")
    k1_d = nc.dram_tensor("k1", [1024, 16384], U8, kind="ExternalInput")
    k2_d = nc.dram_tensor("k2", [1024, 16384], U8, kind="ExternalInput")
    sc_d = nc.dram_tensor("sc", [128, 3], F32, kind="ExternalInput")
    ropec_d = nc.dram_tensor("ropec", [64, TPC], BF16, kind="ExternalInput")
    ropes_d = nc.dram_tensor("ropes", [64, TPC], BF16, kind="ExternalInput")
    mask_d = nc.dram_tensor("maskt", [128, 512], F32, kind="ExternalInput")
    out_d = nc.dram_tensor("out", [TPC, D], F32, kind="ExternalOutput")

    ident_np, prot_np, cvec_np = _host_tables()
    ident_d = nc.inline_tensor(ident_np, "ident")
    prot_d = nc.inline_tensor(prot_np, "prot")
    cvec_d = nc.inline_tensor(cvec_np, "cvec")

    qsp_d = nc.dram_tensor("qsp", [D, TPC], BF16, kind="Internal")
    kc_d = nc.dram_tensor("kcon", [D, TPC], BF16, kind="Internal")
    vc_d = nc.dram_tensor("vcon", [TPC, D], BF16, kind="Internal")
    kg_d = nc.dram_tensor("kgat", [4 * D, TPC], BF16, kind="Internal")
    vg_d = nc.dram_tensor("vgat", [4 * TPC, D], BF16, kind="Internal")
    r1_d = nc.dram_tensor("r1d", [TPC, D], F32, kind="Internal")
    k0b_d = nc.dram_tensor("k0b", [1024, 16384], BF16, kind="Internal")
    k1b_d = nc.dram_tensor("k1b", [1024, 16384], BF16, kind="Internal")
    k2b_d = nc.dram_tensor("k2b", [1024, 16384], BF16, kind="Internal")

    dbg = {}
    if _DEBUG:
        for nm, shp, dt in [
            ("dbg_arena1", [128, 32768], BF16), ("dbg_qsp", [D, TPC], BF16),
            ("dbg_kg", [4 * D, TPC], BF16), ("dbg_vg", [4 * TPC, D], BF16),
            ("dbg_r1", [TPC, D], F32),
        ]:
            dbg[nm] = nc.dram_tensor(nm, shp, dt, kind="ExternalOutput")

    groups = [[0, 1, 2, 3], [4, 5, 6, 7]]

    with ExitStack() as ctx:
        E = ctx.enter_context
        A1 = E(nc.sbuf_tensor("arena1", [128, 65536], BF16))
        A2 = E(nc.sbuf_tensor("arena2", [128, 32768], BF16))
        cols = E(nc.sbuf_tensor("cols", [128, 16], F32))
        colsB = E(nc.sbuf_tensor("colsB", [128, 8], F32))
        scb = E(nc.sbuf_tensor("scb", [128, 3], F32))
        ds0 = E(nc.sbuf_tensor("ds0", [128, 2], F32))
        cvec = E(nc.sbuf_tensor("cvec_sb", [128, 32], F32))
        cs1a = E(nc.sbuf_tensor("cs1a", [128, 32], F32))
        cs1b = E(nc.sbuf_tensor("cs1b", [128, 32], F32))
        cs2a = E(nc.sbuf_tensor("cs2a", [128, 32], F32))
        cs2b = E(nc.sbuf_tensor("cs2b", [128, 32], F32))
        ident = E(nc.sbuf_tensor("ident_sb", [128, 128], BF16))
        prot = E(nc.sbuf_tensor("prot_sb", [128, 128], BF16))
        maskt = E(nc.sbuf_tensor("mask_sb", [128, 512], F32))
        ropec = E(nc.sbuf_tensor("ropec_sb", [64, TPC], BF16))
        ropes = E(nc.sbuf_tensor("ropes_sb", [64, TPC], BF16))
        xsl = [E(nc.sbuf_tensor("xslA", [128, 512], F32)),
               E(nc.sbuf_tensor("xslB", [128, 512], F32))]
        fst = [E(nc.sbuf_tensor("fstA", [128, 512], F32)),
               E(nc.sbuf_tensor("fstB", [128, 512], F32))]
        kconst = E(nc.sbuf_tensor("kconst", [128, 2], F32))
        ones1 = E(nc.sbuf_tensor("ones1", [128, 1], BF16))

        XF = A1[:, 0:32768]
        ATTN = A1[:, 0:32768]
        XF2 = A1[:, 0:32768]
        H2 = A1[:, 0:32768]
        WSs = [A1[:, 32768:36864], A1[:, 36864:40960]]
        WSb = [A1[:, 32768:49152], A1[:, 49152:65536]]
        KH = [A1[:, 32768:40960], A1[:, 40960:49152]]
        VH = [A1[:, 49152:57344], A1[:, 57344:65536]]

        a2f32 = A2.bitcast(F32)
        xt = [a2f32[:, 0:4096], a2f32[:, 4096:8192]]
        SQ = a2f32[:, 8192:12288]
        stg = [a2f32[:, 0:4096], a2f32[:, 4096:8192]]
        S2 = [a2f32[:, 0:4096], a2f32[:, 4096:8192]]
        A2b = [A2[:, 16384:20480], A2[:, 20480:24576]]
        QT = [A2[:, 24576:24832], A2[:, 24832:25088]]
        ATsb = [A2[:, 25088:25600], A2[:, 25600:26112]]
        attn_st = A2[:, 26112:26368]
        xlnbf = A2[:, 26624:30720]
        t64a = a2f32[:, 14336:14848]
        t64b = a2f32[:, 14848:15360]
        rotbf = A2[:, 30720:31232]
        stbf = [A2[:, 31232:31744], A2[:, 31744:32256]]
        H1 = A2[:, 0:32768]

        pbA = E(nc.psum_tensor("pbA", [128, 512], F32))
        pbB = E(nc.psum_tensor("pbB", [128, 512], F32))
        pbC = E(nc.psum_tensor("pbC", [128, 512], F32))
        pbD = E(nc.psum_tensor("pbD", [128, 512], F32))
        pbE = E(nc.psum_tensor("pbE", [128, 1024], BF16))
        pbF = E(nc.psum_tensor("pbF", [128, 1024], BF16))
        pmm = [pbA, pbB]
        pav = [pbC, pbD]
        ptp = [pbE, pbF]

        semnames = ["sdc", "xl0", "xl1", "wl0", "wl1", "kv0", "kv1",
                    "qt0", "qt1", "sl0", "sl1", "st0", "st1",
                    "cst0", "cst1", "cst2",
                    "te", "ve", "sc", "po", "co"]
        sems = {n: E(nc.semaphore(n)) for n in semnames}

        blk = E(nc.Block())

        def flow(f: Fl):
            for dst, src in [
                (ident[:, :], ident_d[:, :]), (prot[:, :], prot_d[:, :]),
                (cvec[:, :], cvec_d[:, :]), (maskt[:, :], mask_d[:, :]),
                (ropec[:, :], ropec_d[:, :]), (ropes[:, :], ropes_d[:, :]),
                (scb[:, :], sc_d[:, :]),
            ]:
                f.do("sy", "sdc", 16, lambda e, d=dst, s=src: e.dma_start(out=d, in_=s))
            sd_const = f.cur("sdc")
            for k_ in ("te", "sc", "gp", "ve"):
                f.wait(k_, "sdc", sd_const)
            for ci, (ksrc, kdst) in enumerate(
                    [(k0_d, k0b_d), (k1_d, k1b_d), (k2_d, k2b_d)]):
                for q in range(4):
                    f.do("gp", "cst%d" % ci, 16,
                         lambda e, ksrc=ksrc, kdst=kdst, q=q: e.dma_start(
                             out=kdst[q * 256:(q + 1) * 256, :],
                             in_=ksrc[q * 256:(q + 1) * 256, :]))
            f.sop("ve", lambda e: e.memset(ones1[:, :], 1.0))
            f.sop("ve", lambda e: e.memset(kconst[:, 0:1], EPS))
            f.sop("ve", lambda e: e.memset(kconst[:, 1:2], 0.0625))
            for (o, i, s1) in [(ds0[:, 0:1], scb[:, 0:1], 2.0),
                               (ds0[:, 1:2], scb[:, 0:1], -1.0)]:
                f.sop("ve", lambda e, o=o, i=i, s1=s1: e.tensor_scalar(
                    out=o, in0=i, scalar1=s1, scalar2=None, op0=ALU.mult))
            for (o, sidx, s2) in [(cs1a, 1, 2.0), (cs1b, 1, -1.0),
                                  (cs2a, 2, 2.0), (cs2b, 2, -1.0)]:
                f.sop("ve", lambda e, o=o, sidx=sidx, s2=s2: e.tensor_scalar(
                    out=o[:, :], in0=cvec[:, :], scalar1=scb[:, sidx:sidx + 1],
                    scalar2=s2, op0=ALU.mult, op1=ALU.mult))
            v_const = f.cur("ve")

            def ln_pass(src_dram, dstFM, stgate):
                loads = []
                lastev_tp = [None, None]
                prev_xln = None
                sqred = {}
                sq_of = {}
                rsx_of = {}
                xln_of = {}
                nmr = {}
                cbs = [cols, colsB]

                def stats(t):
                    cb = cbs[t % 2]
                    xtile = xt[t % 2]
                    xls = "xl%d" % (t % 2)
                    f.wait("gp", xls, loads[t])
                    f.wait("gp", "ve", sqred.get(t - 1))
                    sq_of[t] = f.sop("gp", lambda e, xtile=xtile: e.tensor_tensor(
                        out=SQ[:, :], in0=xtile[:, :], in1=xtile[:, :], op=ALU.mult))
                    f.wait("ve", xls, loads[t])
                    f.wait("ve", "sc", xln_of.get(t - 2))
                    rsx_of[t] = f.sop("ve", lambda e, xtile=xtile, cb=cb: e.reduce_sum(
                        out=cb[:, 1:2], in_=xtile[:, :], axis=AX.X))
                    f.wait("ve", "po", sq_of[t])
                    sqred[t] = f.sop("ve", lambda e, cb=cb: e.reduce_sum(
                        out=cb[:, 0:1], in_=SQ[:, :], axis=AX.X))
                    f.sop("ve", lambda e, cb=cb: e.tensor_scalar(
                        out=cb[:, 2:3], in0=cb[:, 1:2], scalar1=INVD,
                        scalar2=None, op0=ALU.mult))
                    f.sop("ve", lambda e, cb=cb: e.tensor_tensor(
                        out=cb[:, 3:4], in0=cb[:, 2:3], in1=cb[:, 2:3], op=ALU.mult))
                    v_var = f.sop("ve", lambda e, cb=cb: e.scalar_tensor_tensor(
                        out=cb[:, 4:5], in0=cb[:, 0:1], scalar=INVD,
                        in1=cb[:, 3:4], op0=ALU.mult, op1=ALU.subtract))
                    f.wait("sc", "ve", v_var)
                    s_std = f.sop("sc", lambda e, cb=cb: e.activation(
                        out=cb[:, 5:6], in_=cb[:, 4:5], func=AF.Sqrt,
                        bias=kconst[:, 0:1]))
                    f.wait("ve", "sc", s_std)
                    f.sop("ve", lambda e, cb=cb: e.reciprocal(
                        out=cb[:, 6:7], in_=cb[:, 5:6]))
                    nmr[t] = f.sop("ve", lambda e, cb=cb: e.tensor_scalar(
                        out=cb[:, 7:8], in0=cb[:, 2:3], scalar1=cb[:, 6:7],
                        scalar2=-1.0, op0=ALU.mult, op1=ALU.mult))

                for t in range(2):
                    f.waitpair("sy", stgate)
                    if t == 0:
                        f.wait("sy", "ve", v_const)
                    loads.append(f.do("sy", "xl%d" % (t % 2), 16,
                                      lambda e, t=t: e.dma_start(
                                          out=xt[t % 2][:, :],
                                          in_=src_dram[t * 128:(t + 1) * 128, :])))
                stats(0)
                for t in range(NT):
                    cb = cbs[t % 2]
                    xtile = xt[t % 2]
                    xls = "xl%d" % (t % 2)
                    f.wait("sc", "ve", nmr[t])
                    f.wait("sc", "te", prev_xln)
                    s_xln = f.sop("sc",
                                  lambda e, xtile=xtile, cb=cb: e.activation(
                                      out=xlnbf[:, :], in_=xtile[:, :], func=AF.Identity,
                                      scale=cb[:, 6:7], bias=cb[:, 7:8]))
                    xln_of[t] = s_xln
                    if t + 1 < NT:
                        stats(t + 1)
                    if t + 2 < NT:
                        f.wait("sy", "sc", s_xln)
                        f.wait("sy", "po", sq_of[t])
                        f.wait("sy", "ve", rsx_of[t])
                        loads.append(f.do("sy", xls, 16,
                                          lambda e, t=t: e.dma_start(
                                              out=xt[t % 2][:, :],
                                              in_=src_dram[(t + 2) * 128:(t + 3) * 128, :])))
                    f.wait("te", "sc", s_xln)
                    t_last = None
                    for g in range(8):
                        bank = ptp[g % 2]
                        f.wait("te", "sc", lastev_tp[g % 2])
                        def tgrp(e, g=g, bank=bank):
                            for j in range(4):
                                dc = g * 4 + j
                                i = e.transpose(bank[:, j * 128:(j + 1) * 128],
                                                xlnbf[:, dc * 128:(dc + 1) * 128],
                                                ident[:, :])
                            return i
                        t_grp = f.do("te", "te", 1, tgrp)
                        t_last = t_grp
                        f.wait("sc", "te", t_grp)
                        def evac(e, g=g, bank=bank, t=t):
                            dst = dstFM.rearrange("p (dc tt) -> p dc tt", dc=32)[
                                :, g * 4:(g + 1) * 4, t * 128:(t + 1) * 128]
                            return e.activation(
                                out=dst, in_=bank[:, 0:512].rearrange(
                                    "p (j c) -> p j c", j=4), func=AF.Copy)
                        lastev_tp[g % 2] = f.sop("sc", evac)
                    prev_xln = t_last
                return f.cur("sc")

            v_xf_done = ln_pass(x_d, XF, None)

            if _STAGE <= 1:
                if _DEBUG:
                    f.wait("gp", "sc", v_xf_done)
                    f.do("gp", "st0", 16, lambda e: e.dma_start(
                        out=dbg["dbg_arena1"][:, :], in_=A1[:, 0:32768]))
                f.waitpair("gp", f.stpair())
                f.waitpair("sy", f.stpair())
                return

            def fm_proj(wT_d, dst_dram, do_rope, do_scale, v_in_ready):
                te0 = f.cur("te")
                ve0, sc0 = f.cur("ve"), f.cur("sc")
                last_mm_slab = [None, None]
                last_cv_stg = [None, None]
                last_ev_bank = [None, None]
                last_st_stage = [f.cur("st0"), f.cur("st1")]
                last_v_rot = None
                for ms in range(32):
                    buf = WSs[ms % 2]
                    wls = "wl%d" % (ms % 2)
                    if ms < 2:
                        f.wait("sy", "ve", ve0)
                        f.wait("sy", "sc", sc0)
                    f.wait("sy", "ve", last_cv_stg[ms % 2])
                    gl = f.do("sy", wls, 16,
                              lambda e, ms=ms: e.dma_start(
                                  out=stg[ms % 2][:, :],
                                  in_=wT_d[ms * 128:(ms + 1) * 128, :]))
                    f.wait("ve", wls, gl)
                    f.wait("ve", "te", last_mm_slab[ms % 2] or te0)
                    v_cv = f.sop("ve", lambda e, buf=buf, ms=ms: e.tensor_copy(
                        out=buf[:, :], in_=stg[ms % 2][:, :]))
                    last_cv_stg[ms % 2] = v_cv
                    for ts2 in range(2):
                        bank = pmm[ts2]
                        sts = "st%d" % ts2
                        f.wait("te", "ve", v_cv)
                        if ms == 0 and ts2 == 0:
                            f.wait("te", "sc", v_in_ready)
                        f.wait("te", "sc", last_ev_bank[ts2])
                        def mm(e, buf=buf, bank=bank, ts2=ts2):
                            for dc in range(32):
                                i = e.matmul(
                                    bank[:, :], buf[:, dc * 128:(dc + 1) * 128],
                                    XF[:, dc * 1024 + ts2 * 512: dc * 1024 + ts2 * 512 + 512],
                                    start=(dc == 0), stop=(dc == 31))
                            return i
                        t_mm = f.do("te", "te", 1, mm)
                        last_mm_slab[ms % 2] = t_mm
                        st = stbf[ts2]
                        f.wait("sc", sts, last_st_stage[ts2])
                        f.wait("sc", "te", t_mm)
                        s_ev = f.sop("sc",
                                     lambda e, st=st, bank=bank, do_scale=do_scale: e.activation(
                                         out=st[:, :], in_=bank[:, :], func=AF.Copy,
                                         scale=(kconst[:, 1:2] if do_scale else 1.0)))
                        last_ev_bank[ts2] = s_ev
                        prod_sem, prod_val = "sc", s_ev
                        if do_rope and (ms % 2 == 0):
                            f.wait("te", "sc", s_ev)
                            f.wait("te", "ve", last_v_rot)
                            t_rot = f.do("te", "te", 1,
                                         lambda e, st=st: e.matmul(
                                             pbC[:, :], prot[:, :], st[:, :],
                                             start=True, stop=True))
                            f.wait("sc", "te", t_rot)
                            s_rb = f.sop("sc",
                                         lambda e: e.activation(out=rotbf[0:64, :],
                                                                in_=pbC[0:64, :],
                                                                func=AF.Copy))
                            last_ev_bank[ts2] = s_rb
                            f.wait("ve", "sc", s_ev)
                            f.sop("ve", lambda e, st=st, ts2=ts2: e.tensor_tensor(
                                out=t64a[0:64, :], in0=st[0:64, :],
                                in1=ropec[:, ts2 * 512:(ts2 + 1) * 512], op=ALU.mult))
                            f.wait("ve", "sc", s_rb)
                            f.sop("ve", lambda e, ts2=ts2: e.tensor_tensor(
                                out=t64b[0:64, :], in0=rotbf[0:64, :],
                                in1=ropes[:, ts2 * 512:(ts2 + 1) * 512], op=ALU.mult))
                            v_rot = f.sop("ve",
                                          lambda e, st=st: e.tensor_tensor(
                                              out=st[0:64, :], in0=t64a[0:64, :],
                                              in1=t64b[0:64, :], op=ALU.add))
                            last_v_rot = v_rot
                            prod_sem, prod_val = "ve", v_rot
                        f.wait("sc", prod_sem, prod_val)
                        g_st = f.do("sc", sts, 16,
                                    lambda e, st=st, ms=ms, ts2=ts2, dd=dst_dram: e.dma_start(
                                        out=dd[ms * 128:(ms + 1) * 128,
                                               ts2 * 512:(ts2 + 1) * 512],
                                        in_=st[:, :]))
                        last_st_stage[ts2] = g_st
                return f.stpair()

            kp = fm_proj(wkT_d, kc_d, do_rope=True, do_scale=False,
                         v_in_ready=v_xf_done)
            f.waitpair("gp", kp)
            co_k = f.do("gp", "co", 1,
                        lambda e: e.collective_compute(
                            "AllGather", ALU.bypass, replica_groups=groups,
                            ins=[kc_d.ap().opt()], outs=[kg_d.ap().opt()]))

            def tm_proj(wT_d, dst_dram, src_act, r1_add, te_ready, ve_ready):
                te0 = f.cur("te")
                ve0, sc0, po0 = f.cur("ve"), f.cur("sc"), f.cur("po")
                last_mm_slab = [None, None]
                last_cv_stg = [None, None]
                last_ev_bank = [None, None]
                last_st_stage = [f.cur("st0"), f.cur("st1")]
                last_sl_ev = [None, None]
                evsem = "ve" if r1_add is not None else "sc"
                for ms in range(8):
                    buf = WSb[ms % 2]
                    v_cv = None
                    for q in range(4):
                        sslot = q % 2
                        wls = "wl%d" % sslot
                        if ms == 0 and q < 2:
                            f.wait("sy", "ve", ve0)
                            f.wait("sy", "sc", sc0)
                            f.wait("sy", "po", po0)
                        f.wait("sy", "ve", last_cv_stg[sslot])
                        gl = f.do("sy", wls, 16,
                                  lambda e, ms=ms, q=q, sslot=sslot: e.dma_start(
                                      out=stg[sslot][:, :],
                                      in_=wT_d[ms * 128:(ms + 1) * 128,
                                               q * 4096:(q + 1) * 4096]))
                        f.wait("ve", wls, gl)
                        if q < 2:
                            f.wait("ve", "te", last_mm_slab[ms % 2] or te0)
                        v_cv = f.sop("ve", lambda e, buf=buf, q=q, sslot=sslot: e.tensor_copy(
                            out=buf[:, q * 4096:(q + 1) * 4096], in_=stg[sslot][:, :]))
                        last_cv_stg[sslot] = v_cv
                    for tt in range(NT):
                        bank = pmm[tt % 2]
                        sts = "st%d" % (tt % 2)
                        sls = "sl%d" % (tt % 2)
                        sl = None
                        if r1_add is not None:
                            xs = xsl[tt % 2]
                            f.wait("sy", "ve", last_sl_ev[tt % 2])
                            sl = f.do("sy", sls, 16,
                                      lambda e, xs=xs, ms=ms, tt=tt: e.dma_start(
                                          out=xs[:, :],
                                          in_=r1_add[tt * 128:(tt + 1) * 128,
                                                     ms * 512:(ms + 1) * 512]))
                        f.wait("te", "ve", v_cv)
                        if ms == 0 and tt == 0:
                            f.wait("te", ve_ready[0], ve_ready[1])
                            f.wait("te", "te", te_ready)
                        f.wait("te", evsem, last_ev_bank[tt % 2])
                        def mmv(e, buf=buf, bank=bank, tt=tt):
                            for dc in range(32):
                                i = e.matmul(
                                    bank[:, :],
                                    src_act[:, dc * 1024 + tt * 128: dc * 1024 + tt * 128 + 128],
                                    buf[:, dc * 512:(dc + 1) * 512],
                                    start=(dc == 0), stop=(dc == 31))
                            return i
                        t_mm = f.do("te", "te", 1, mmv)
                        last_mm_slab[ms % 2] = t_mm
                        if r1_add is None:
                            st = stbf[tt % 2]
                            f.wait("sc", sts, last_st_stage[tt % 2])
                            f.wait("sc", "te", t_mm)
                            s_ev = f.sop("sc",
                                         lambda e, st=st, bank=bank: e.activation(
                                             out=st[:, :], in_=bank[:, :], func=AF.Copy))
                            last_ev_bank[tt % 2] = s_ev
                            f.wait("sc", "sc", s_ev)
                            g_st = f.do("sc", sts, 16,
                                        lambda e, st=st, ms=ms, tt=tt: e.dma_start(
                                            out=dst_dram[tt * 128:(tt + 1) * 128,
                                                         ms * 512:(ms + 1) * 512],
                                            in_=st[:, :]))
                        else:
                            fs = fst[tt % 2]
                            f.wait("ve", sts, last_st_stage[tt % 2])
                            f.wait("ve", "te", t_mm)
                            f.wait("ve", sls, sl)
                            v_ev = f.sop("ve",
                                         lambda e, bank=bank, xs=xs, fs=fs: e.scalar_tensor_tensor(
                                             out=fs[:, :], in0=bank[:, :], scalar=1.0,
                                             in1=xs[:, :], op0=ALU.mult, op1=ALU.add))
                            last_ev_bank[tt % 2] = v_ev
                            last_sl_ev[tt % 2] = v_ev
                            f.wait("sc", "ve", v_ev)
                            g_st = f.do("sc", sts, 16,
                                        lambda e, fs=fs, ms=ms, tt=tt: e.dma_start(
                                            out=dst_dram[tt * 128:(tt + 1) * 128,
                                                         ms * 512:(ms + 1) * 512],
                                            in_=fs[:, :]))
                        last_st_stage[tt % 2] = g_st
                return f.stpair()

            vp = tm_proj(wvT_d, vc_d, XF, None, 0, ("sc", v_xf_done))
            f.waitpair("gp", vp)
            co_v = f.do("gp", "co", 1,
                        lambda e: e.collective_compute(
                            "AllGather", ALU.bypass, replica_groups=groups,
                            ins=[vc_d.ap().opt()], outs=[vg_d.ap().opt()]))

            qp = fm_proj(wqT_d, qsp_d, do_rope=True, do_scale=True,
                         v_in_ready=v_xf_done)
            te_q_done = f.cur("te")

            if _STAGE <= 2:
                if _DEBUG:
                    f.waitpair("gp", f.stpair())
                    f.wait("gp", "co", f.cur("co"))
                    f.wait("gp", "te", te_q_done)
                    for nm, src in [("dbg_arena1", A1[:, 0:32768]),
                                    ("dbg_qsp", qsp_d[:, :]),
                                    ("dbg_kg", kg_d[:, :]), ("dbg_vg", vg_d[:, :])]:
                        f.do("gp", "st0", 16,
                             lambda e, nm=nm, src=src: e.dma_start(out=dbg[nm][:, :], in_=src))
                f.waitpair("gp", f.stpair())
                f.waitpair("sy", f.stpair())
                return

            kv_loaded = {}
            kv_users = {}

            def load_head(h):
                par = h % 2
                kvs = "kv%d" % par
                f.wait("sy", "te", kv_users.get(h - 2))
                if h <= 1:
                    f.wait("sy", "co", co_v)
                    f.wait("sy", "te", te_q_done)
                    f.waitpair("sy", qp)
                last = None
                for dc in range(2):
                    for sb in range(4):
                        srcrow = sb * 4096 + h * 256 + dc * 128
                        dst = KH[par][:, dc * 4096 + sb * 1024: dc * 4096 + (sb + 1) * 1024]
                        last = f.do("sy", kvs, 16,
                                    lambda e, d=dst, srcrow=srcrow: e.dma_start(
                                        out=d, in_=kg_d[srcrow:srcrow + 128, :]))
                for sb in range(4):
                    src = vg_d[sb * 1024:(sb + 1) * 1024, h * 256:(h + 1) * 256].rearrange(
                        "(kc p) c -> p kc c", p=128)
                    dst = VH[par].rearrange("p (kc four c) -> p kc four c",
                                            four=4, c=256)[:, :, sb, :]
                    last = f.do("sy", kvs, 16,
                                lambda e, d=dst, s=src: e.dma_start(out=d, in_=s))
                kv_loaded[h] = last

            load_head(0)
            load_head(1)

            last_qt_user = [None, None]
            last_bank_ev = [None, None]
            last_at_ev = [None, None]
            info = {}
            prev_v_attn = None
            last_tattn = None

            def rnd(r):
                return (r // 8, r % 8, r % 2)

            for r in range(130):
                if r < 128:
                    h, k, pr = rnd(r)
                    par = h % 2
                    kvs = "kv%d" % par
                    qts = "qt%d" % (r % 2)
                    f.wait("sy", "te", last_qt_user[r % 2])
                    ql = None
                    for dc in range(2):
                        ql = f.do("sy", qts, 16,
                                  lambda e, h=h, k=k, r=r, dc=dc: e.dma_start(
                                      out=QT[r % 2][:, dc * 128:(dc + 1) * 128],
                                      in_=qsp_d[h * 256 + dc * 128: h * 256 + dc * 128 + 128,
                                                k * 128:(k + 1) * 128]))
                    ev_diag = None
                    evs_act = None
                    t_s = None
                    for kc in range(k + 1):
                        bank = pmm[kc % 2]
                        if last_bank_ev[kc % 2]:
                            f.wait("te", *last_bank_ev[kc % 2])
                        f.wait("te", qts, ql)
                        if kc == 0:
                            f.wait("te", kvs, kv_loaded[h])
                        def smm(e, bank=bank, kc=kc, r=r, par=par):
                            for dc in range(2):
                                rhs = KH[par][:, dc * 4096:(dc + 1) * 4096].rearrange(
                                    "p (sb c) -> p sb c", sb=4)[:, :, kc * 128:(kc + 1) * 128]
                                i = e.matmul(
                                    bank[:, :], QT[r % 2][:, dc * 128:(dc + 1) * 128],
                                    rhs, start=(dc == 0), stop=(dc == 1))
                            return i
                        t_s = f.do("te", "te", 1, smm)
                        sx = info.get(r - 2)
                        if kc == k:
                            f.wait("ve", "te", t_s)
                            if sx:
                                f.wait("ve", "sc", sx["exp"])
                            ev_diag = f.sop("ve",
                                            lambda e, bank=bank, kc=kc, pr=pr: e.tensor_tensor(
                                                out=S2[pr][:, kc * 512:(kc + 1) * 512],
                                                in0=bank[:, :], in1=maskt[:, :], op=ALU.add))
                            last_bank_ev[kc % 2] = ("ve", ev_diag)
                        else:
                            f.wait("sc", "te", t_s)
                            if kc == 0 and sx:
                                f.wait("sc", "sc", sx["exp"])
                            evs_act = f.sop("sc",
                                            lambda e, bank=bank, kc=kc, pr=pr: e.activation(
                                                out=S2[pr][:, kc * 512:(kc + 1) * 512],
                                                in_=bank[:, :], func=AF.Copy))
                            last_bank_ev[kc % 2] = ("sc", evs_act)
                    last_qt_user[r % 2] = t_s
                    info[r] = {"k": k, "pr": pr, "par": par,
                               "ev_diag": ev_diag, "evs_act": evs_act}

                if r < 128:
                    h, k, pr = rnd(r)
                    L = (k + 1) * 512
                    inf = info[r]
                    if inf["evs_act"] is not None:
                        f.wait("ve", "sc", inf["evs_act"])
                    v_max = f.sop("ve", lambda e, L=L, pr=pr: e.reduce_max(
                        out=cols[:, 8 + pr:9 + pr], in_=S2[pr][:, 0:L],
                        axis=AX.X, negate=True))
                    f.wait("sc", "ve", v_max)
                    prev = info.get(r - 2)
                    if prev:
                        f.wait("sc", "te", prev.get("t_at_last"))
                    s_exp = f.sop("sc",
                                  lambda e, L=L, pr=pr: e.activation(
                                      out=A2b[pr][:, 0:L], in_=S2[pr][:, 0:L], func=AF.Exp,
                                      bias=cols[:, 8 + pr:9 + pr]))
                    inf["exp"] = s_exp

                if 1 <= r <= 128:
                    h1_, k1_, pr1 = rnd(r - 1)
                    par1 = h1_ % 2
                    inf1 = info[r - 1]
                    atev_of = {}
                    t_at_last = None
                    t_av = None
                    for kc in range(k1_ + 1):
                        tb = ptp[kc % 2]
                        f.wait("te", "ve", last_at_ev[kc % 2])
                        if kc == 0:
                            f.wait("te", "sc", inf1["exp"])
                            f.wait("te", "ve", prev_v_attn)
                        def tat(e, tb=tb, kc=kc, pr1=pr1):
                            for j in range(4):
                                c0 = kc * 512 + j * 128
                                i = e.transpose(tb[:, j * 128:(j + 1) * 128],
                                                A2b[pr1][:, c0:c0 + 128], ident[:, :])
                            return i
                        t_at = f.do("te", "te", 1, tat)
                        t_at_last = t_at
                        f.wait("ve", "te", t_at)
                        v_atev = f.sop("ve",
                                       lambda e, tb=tb, kc=kc: e.tensor_copy(
                                           out=ATsb[kc % 2][:, :], in_=tb[:, 0:512]))
                        last_at_ev[kc % 2] = v_atev
                        atev_of[kc] = v_atev
                        if kc >= 1:
                            f.wait("te", "ve", atev_of[kc - 1])
                            def avm(e, kc=kc - 1, par1=par1, pr1=pr1):
                                for j in range(4):
                                    piece = kc * 4 + j
                                    e.matmul(
                                        pav[pr1][:, 0:256],
                                        ATsb[kc % 2][:, j * 128:(j + 1) * 128],
                                        VH[par1][:, piece * 256:(piece + 1) * 256],
                                        start=(kc == 0 and j == 0), stop=False)
                                    i = e.matmul(
                                        pav[pr1][:, 256:257],
                                        ATsb[kc % 2][:, j * 128:(j + 1) * 128],
                                        ones1[:, :], start=False, stop=False)
                                return i
                            f.do("te", "te", 1, avm)
                    f.wait("te", "ve", atev_of[k1_])
                    def avml(e, kc=k1_, par1=par1, pr1=pr1):
                        for j in range(4):
                            piece = kc * 4 + j
                            e.matmul(
                                pav[pr1][:, 0:256],
                                ATsb[kc % 2][:, j * 128:(j + 1) * 128],
                                VH[par1][:, piece * 256:(piece + 1) * 256],
                                start=(kc == 0 and j == 0), stop=False)
                            i = e.matmul(
                                pav[pr1][:, 256:257],
                                ATsb[kc % 2][:, j * 128:(j + 1) * 128],
                                ones1[:, :], start=False, stop=(j == 3))
                        return i
                    t_av = f.do("te", "te", 1, avml)
                    inf1["t_av"] = t_av
                    inf1["t_at_last"] = t_at_last
                    if k1_ == 7:
                        kv_users[h1_] = t_av
                        if h1_ + 2 < NH:
                            load_head(h1_ + 2)

                if 2 <= r <= 129:
                    inf2 = info[r - 2]
                    pr2 = inf2["pr"]
                    h2_, k2_, _ = rnd(r - 2)
                    f.wait("ve", "te", inf2["t_av"])
                    f.wait("ve", "te", last_tattn)
                    f.sop("ve", lambda e, pr2=pr2: e.reciprocal(
                        out=cols[:, 10 + pr2:11 + pr2], in_=pav[pr2][:, 256:257]))
                    v_at = f.sop("ve",
                                 lambda e, pr2=pr2: e.tensor_scalar(
                                     out=attn_st[:, :], in0=pav[pr2][:, 0:256],
                                     scalar1=cols[:, 10 + pr2:11 + pr2], scalar2=None,
                                     op0=ALU.mult))
                    f.wait("te", "ve", v_at)
                    def tattn(e):
                        for dh in range(2):
                            i = e.transpose(pbE[:, 512 + dh * 128: 512 + (dh + 1) * 128],
                                            attn_st[:, dh * 128:(dh + 1) * 128],
                                            ident[:, :])
                        return i
                    t_ta = f.do("te", "te", 1, tattn)
                    last_tattn = t_ta
                    f.wait("ve", "te", t_ta)
                    def evat(e, h2_=h2_, k2_=k2_):
                        dst = ATTN.rearrange("p (hc t) -> p hc t", hc=32)[
                            :, 2 * h2_:2 * h2_ + 2, k2_ * 128:(k2_ + 1) * 128]
                        return e.tensor_copy(out=dst, in_=pbE[:, 512:768].rearrange(
                            "p (dh c) -> p dh c", dh=2))
                    prev_v_attn = f.sop("ve", evat)

                info.pop(r - 3, None)

            v_attn_done = f.cur("ve")
            te_attn_done = f.cur("te")

            r1p = tm_proj(woT_d, r1_d, ATTN, x_d, te_attn_done,
                          ("ve", v_attn_done))
            v_xf2_done = ln_pass(r1_d, XF2, r1p)

            if _STAGE <= 4:
                if _DEBUG:
                    f.waitpair("gp", f.stpair())
                    f.wait("gp", "sc", v_xf2_done)
                    for nm, src in [("dbg_qsp", qsp_d[:, :]), ("dbg_kg", kg_d[:, :]),
                                    ("dbg_vg", vg_d[:, :]), ("dbg_r1", r1_d[:, :]),
                                    ("dbg_arena1", A1[:, 0:32768])]:
                        f.do("gp", "st0", 16,
                             lambda e, nm=nm, src=src: e.dma_start(out=dbg[nm][:, :], in_=src))
                f.waitpair("gp", f.stpair())
                f.waitpair("sy", f.stpair())
                return

            def ffn_fm(kb_d, csem, sa, sb_, src_act, dstH, sc_in_ready, ve_in_ready):
                te0 = f.cur("te")
                last_mm_slab = [None, None]
                last_ev_bank = [None, None]
                for jsl in range(8):
                    buf = WSb[jsl % 2]
                    wls = "wl%d" % (jsl % 2)
                    f.wait("sy", "te", last_mm_slab[jsl % 2] or te0)
                    f.wait("sy", csem, 64)
                    gl = f.do("sy", wls, 16,
                              lambda e, buf=buf, jsl=jsl, kb_d=kb_d: e.dma_start(
                                  out=buf[:, :], in_=kb_d[jsl * 128:(jsl + 1) * 128, :]))
                    f.wait("ve", wls, gl)
                    f.wait("ve", "te", last_mm_slab[jsl % 2] or te0)
                    def conv(e, buf=buf):
                        for ic in range(32):
                            s1 = sa[:, ic:ic + 1] if sa.shape[1] == 32 else sa
                            s2 = sb_[:, ic:ic + 1] if sb_.shape[1] == 32 else sb_
                            i = e.tensor_scalar(out=buf[:, ic * 512:(ic + 1) * 512],
                                                in0=buf[:, ic * 512:(ic + 1) * 512],
                                                scalar1=s1, scalar2=s2,
                                                op0=ALU.mult, op1=ALU.add)
                        return i
                    v_cv = f.sop("ve", conv)
                    for j1 in range(4):
                        for ts2 in range(2):
                            bank = pmm[ts2]
                            f.wait("te", "ve", v_cv)
                            if jsl == 0 and j1 == 0 and ts2 == 0:
                                f.wait("te", "sc", sc_in_ready)
                                f.wait("te", ve_in_ready[0], ve_in_ready[1])
                            f.wait("te", "sc", last_ev_bank[ts2])
                            def mmf(e, buf=buf, bank=bank, j1=j1, ts2=ts2):
                                for ic in range(32):
                                    i = e.matmul(
                                        bank[:, :],
                                        buf[:, ic * 512 + j1 * 128: ic * 512 + j1 * 128 + 128],
                                        src_act[:, ic * 1024 + ts2 * 512: ic * 1024 + ts2 * 512 + 512],
                                        start=(ic == 0), stop=(ic == 31))
                                return i
                            t_mm = f.do("te", "te", 1, mmf)
                            last_mm_slab[jsl % 2] = t_mm
                            f.wait("sc", "te", t_mm)
                            jc = jsl * 4 + j1
                            s_ev = f.sop("sc",
                                         lambda e, bank=bank, jc=jc, ts2=ts2: e.activation(
                                             out=dstH[:, jc * 1024 + ts2 * 512:
                                                      jc * 1024 + ts2 * 512 + 512],
                                             in_=bank[:, :], func=AF.Relu))
                            last_ev_bank[ts2] = s_ev
                return f.cur("sc")

            sc_w0_done = ffn_fm(k0b_d, "cst0", ds0[:, 0:1], ds0[:, 1:2], XF2, H1,
                                f.cur("sc"), ("sc", v_xf2_done))
            sc_w1_done = ffn_fm(k1b_d, "cst1", cs1a, cs1b, H1, H2, sc_w0_done,
                                ("ve", 0))

            def ffn_out():
                te0 = f.cur("te")
                last_mm_slab = [None, None]
                last_ev_bank = [None, None]
                last_st_stage = [f.cur("st0"), f.cur("st1")]
                last_sl = [None, None]
                for jsl in range(8):
                    buf = WSb[jsl % 2]
                    wls = "wl%d" % (jsl % 2)
                    f.wait("sy", "te", last_mm_slab[jsl % 2] or te0)
                    f.wait("sy", "cst2", 64)
                    gl = f.do("sy", wls, 16,
                              lambda e, buf=buf, jsl=jsl: e.dma_start(
                                  out=buf[:, :], in_=k2b_d[jsl * 128:(jsl + 1) * 128, :]))
                    f.wait("ve", wls, gl)
                    f.wait("ve", "te", last_mm_slab[jsl % 2] or te0)
                    def conv(e, buf=buf):
                        for ic in range(32):
                            i = e.tensor_scalar(out=buf[:, ic * 512:(ic + 1) * 512],
                                                in0=buf[:, ic * 512:(ic + 1) * 512],
                                                scalar1=cs2a[:, ic:ic + 1],
                                                scalar2=cs2b[:, ic:ic + 1],
                                                op0=ALU.mult, op1=ALU.add)
                        return i
                    v_cv = f.sop("ve", conv)
                    for tt in range(NT):
                        bank = pmm[tt % 2]
                        sts = "st%d" % (tt % 2)
                        sls = "sl%d" % (tt % 2)
                        rs = xsl[tt % 2]
                        f.wait("sy", "ve", last_sl[tt % 2])
                        sl = f.do("sy", sls, 16,
                                  lambda e, rs=rs, jsl=jsl, tt=tt: e.dma_start(
                                      out=rs[:, :],
                                      in_=r1_d[tt * 128:(tt + 1) * 128,
                                               jsl * 512:(jsl + 1) * 512]))
                        f.wait("te", "ve", v_cv)
                        if jsl == 0 and tt == 0:
                            f.wait("te", "sc", sc_w1_done)
                        f.wait("te", "ve", last_ev_bank[tt % 2])
                        def mmf2(e, buf=buf, bank=bank, tt=tt):
                            for ic in range(32):
                                i = e.matmul(
                                    bank[:, :],
                                    H2[:, ic * 1024 + tt * 128: ic * 1024 + tt * 128 + 128],
                                    buf[:, ic * 512:(ic + 1) * 512],
                                    start=(ic == 0), stop=(ic == 31))
                            return i
                        t_mm = f.do("te", "te", 1, mmf2)
                        last_mm_slab[jsl % 2] = t_mm
                        fs = fst[tt % 2]
                        f.wait("ve", sts, last_st_stage[tt % 2])
                        f.wait("ve", "te", t_mm)
                        f.wait("ve", sls, sl)
                        v_ev = f.sop("ve",
                                     lambda e, bank=bank, rs=rs, fs=fs: e.scalar_tensor_tensor(
                                         out=fs[:, :], in0=bank[:, :], scalar=1.0,
                                         in1=rs[:, :], op0=ALU.mult, op1=ALU.add))
                        last_ev_bank[tt % 2] = v_ev
                        last_sl[tt % 2] = v_ev
                        f.wait("sc", "ve", v_ev)
                        g_st = f.do("sc", sts, 16,
                                    lambda e, fs=fs, jsl=jsl, tt=tt: e.dma_start(
                                        out=out_d[tt * 128:(tt + 1) * 128,
                                                  jsl * 512:(jsl + 1) * 512],
                                        in_=fs[:, :]))
                        last_st_stage[tt % 2] = g_st
                return

            ffn_out()

            if _DEBUG:
                f.waitpair("gp", f.stpair())
                for nm, src in [("dbg_qsp", qsp_d[:, :]), ("dbg_kg", kg_d[:, :]),
                                ("dbg_vg", vg_d[:, :]), ("dbg_r1", r1_d[:, :])]:
                    f.do("gp", "st0", 16,
                         lambda e, nm=nm, src=src: e.dma_start(out=dbg[nm][:, :], in_=src))

            f.waitpair("gp", f.stpair())
            f.waitpair("sy", f.stpair())

        engmap = {"sy": "sync", "gp": "gpsimd", "sc": "scalar",
                  "ve": "vector", "te": "tensor"}
        for key, attr in engmap.items():
            def body(eng, key=key):
                flow(Fl(key, eng, sems))
            getattr(blk, attr)(body)

    return nc


_NC_CACHE = {}


def kernel(L0_attn_hidden_states, L0_attn_wq, L0_attn_wk, L0_attn_wv, L0_attn_wo,
           L0_ffn_k0, L0_ffn_s0, L0_ffn_k1, L0_ffn_s1, L0_ffn_k2, L0_ffn_s2):
    x = np.asarray(L0_attn_hidden_states)

    def swz128(wT):
        return np.ascontiguousarray(
            wT.reshape(32, 128, 32, 128).transpose(2, 1, 0, 3).reshape(4096, 4096))

    def swz512(wT):
        return np.ascontiguousarray(
            wT.reshape(32, 128, 8, 512).transpose(2, 1, 0, 3).reshape(1024, 16384))

    wqT = swz128(np.asarray(L0_attn_wq).T)
    wkT = swz128(np.asarray(L0_attn_wk).T)
    wvT = swz512(np.asarray(L0_attn_wv).T)
    woT = swz512(np.asarray(L0_attn_wo).T)
    k0 = swz512(np.asarray(L0_ffn_k0).view(np.uint8))
    k1 = swz512(np.asarray(L0_ffn_k1).view(np.uint8))
    k2 = swz512(np.asarray(L0_ffn_k2).view(np.uint8))
    sc = np.ascontiguousarray(np.broadcast_to(
        np.stack([np.float32(L0_ffn_s0), np.float32(L0_ffn_s1),
                  np.float32(L0_ffn_s2)]).astype(np.float32)[None, :], (128, 3)))

    if "nc" not in _NC_CACHE:
        _NC_CACHE["nc"] = build_program()
    nc = _NC_CACHE["nc"]

    in_maps = []
    for c in range(NCORE):
        seq, sub = c // 4, c % 4
        xs = np.concatenate([x[seq, k * 512 + sub * 128: k * 512 + sub * 128 + 128]
                             for k in range(NT)], axis=0)
        cos, sin = _host_rope(sub)
        in_maps.append({
            "x": np.ascontiguousarray(xs),
            "wqT": wqT, "wkT": wkT, "wvT": wvT, "woT": woT,
            "k0": k0, "k1": k1, "k2": k2, "sc": sc,
            "ropec": np.ascontiguousarray(cos), "ropes": np.ascontiguousarray(sin),
            "maskt": _host_mask(sub),
        })

    res = run_bass_kernel_spmd(nc, in_maps, core_ids=list(range(NCORE)))
    out = np.empty((B, S, D), dtype=np.float32)
    for c in range(NCORE):
        seq, sub = c // 4, c % 4
        oc = res.results[c]["out"]
        for k in range(NT):
            out[seq, k * 512 + sub * 128: k * 512 + sub * 128 + 128] = \
                oc[k * 128:(k + 1) * 128]
    if _DEBUG:
        kernel._dbg = res.results
    return out
